# version 15
# baseline (speedup 1.0000x reference)
import numpy as np
from contextlib import ExitStack

import concourse.bass as bass
import concourse.mybir as mybir
from concourse.bass_utils import run_bass_kernel_spmd

F32 = mybir.dt.float32
BF16 = mybir.dt.bfloat16
I8 = mybir.dt.int8
U8 = mybir.dt.uint8
AF = mybir.ActivationFunctionType
ALU = mybir.AluOpType

NCORE = 8
NTOK = 4096
T = 512
NT = NTOK // T
NCH = 16
H = 16
KVH = 4
EPS_LN = 1e-5
EPS_ATTN = 1e-6
INV_D = 1.0 / 2048.0

A = 0
R2 = 98304
R3 = 163840
R4 = 196608
ARENA_BYTES = R4 + 15104

O_WQ = A + 0
O_WK = A + 65536
O_WV = A + 81920
O_WO = A + 0
O_SQ2 = (A + 65536, A + 81920)
O_F = A + 0
O_G = (A + 65536, A + 73728)
O_U = (A + 81920, A + 86016)
O_WD = (A + 65536, A + 73728)
O_X1SQ = tuple(163840 + 22528 + 1024 * i for i in range(4))
O_DSQ4 = tuple(163840 + 26624 + 1024 * i for i in range(4))
O_RBX3 = A + 92160
O_NBX3 = A + 93184
O_ROWS3 = A + 94208
O_XB = (R2 + 0, R2 + 16384)
O_XN = R2 + 32768
O_SQ1 = R2 + 49152
O_X2IN = R2 + 0
O_ROWS2 = R2 + 32768
O_TMP2 = (R4 + 11776, R4 + 12800)
O_ATTB = R2 + 43008
O_X1F = R2 + 0
O_X1B = R2 + 32768
O_X1N = R2 + 49152
O_SK = R3 + 0
O_VX = R3 + 4096
O_KVE = R3 + 8704
O_ROWS1 = R3 + 11264
O_EQMN = (R3 + 19456, R3 + 21504)
O_EQEX = (R3 + 23552, R3 + 24576)
O_EQMX = (R3 + 25600, R3 + 26624)
O_APT1 = (R3 + 27648, R3 + 28672)
O_ATT = R3 + 0
O_AOB = R3 + 16384
O_DS = R3 + 0
O_US = (R3 + 16384, R3 + 17408)
O_FS = (R3 + 18432, R3 + 19456)
O_X3T = (R3 + 20480, R3 + 21504)
O_XSQ = (R4 + 0, R4 + 1024)
O_RBX1 = R4 + 2048
O_NBX1 = R4 + 3072
O_AOSQ = (R4 + 0, R4 + 1024)
O_X1C = (R4 + 2048, R4 + 4096)
O_RB2 = R4 + 6144
O_NB2 = R4 + 7168
O_KVSB = R4 + 10240
O_S1 = R4 + 4096
O_OB = (R4 + 6144, R4 + 8192)
O_RBD = R4 + 10240
O_NBD = R4 + 11264
O_RL3 = R4 + 12288

RW1_BQ = O_ROWS1 + 0
RW1_RBF = O_ROWS1 + 4096
RW1_NBF = O_ROWS1 + 5120
RW1_ONE = O_ROWS1 + 6144
RW1_M = O_ROWS1 + 0
RW1_Q = O_ROWS1 + 2048
RW1_L = O_ROWS1 + 4096
RW1_R = O_ROWS1 + 6144
RW1_BK = O_ROWS1 + 7168
RW1_BV = R4 + 4096
RW1_STG = O_ROWS1 + 0
RW2_RBF = O_ROWS2 + 0
RW2_NBF = O_ROWS2 + 1024
RW2_RECBF = O_ROWS2 + 2048
RW2_RECBF2 = O_ROWS2 + 3072
RW2_M = O_ROWS2 + 0
RW2_Q = O_ROWS2 + 2048
RW2_L = O_ROWS2 + 4096
RW2_R = O_ROWS2 + 6144
RW2_NORM = O_ROWS2 + 0
RW2_REC = O_ROWS2 + 2048
RW3_DRBF = O_ROWS3 + 0
RW3_DNBF = O_ROWS3 + 1024
RW3_XRBF = O_ROWS3 + 2048
RW3_XNBF = O_ROWS3 + 3072
RW3_M = O_ROWS3 + 0
RW3_Q = O_ROWS3 + 2048
RW3_RR = R4 + 2048


class _Sem:
    __slots__ = ("h", "count", "name")
    def __init__(self, name):
        self.h = None
        self.count = 0
        self.name = name


class Bld:
    ROLES = ("sync", "gpsimd", "tensor", "scalar", "vector")

    DMA_SEMS = ("sGpre", "sGx0", "sGx1", "sGkv", "sGdbg", "sGwo0", "sGwo1",
                "sGx1b", "sGx1b2", "sGg0", "sGg1", "sGu", "sGwd0", "sGwd1",
                "sSpre", "sSsq", "sSq20", "sSq21", "sSx", "sSx1f",
                "sSx10", "sSx11", "sSx12", "sSx13", "sSo0", "sSo1", "sSo2", "sSo3", "sScc", "sGu1")

    def __init__(self):
        self.events = {}
        self.dry = True
        self.role = None
        self.eng = None
        self.sems = {n: _Sem(n) for n in
                     ("sM", "sA", "sV", "sP") + self.DMA_SEMS}
        self._cur_dma_n = {n: 0 for n in self.DMA_SEMS}
        self.streams = {r: [] for r in self.ROLES}

    def begin(self, role, eng):
        self.role = role
        self.eng = eng
        for s in self.sems.values():
            s.count = 0
        self._cur_dma_n = {n: 0 for n in self.DMA_SEMS}

    def _note_inc(self, sem, n, ev, role):
        sem.count += n
        if ev is not None:
            if self.dry:
                assert ev not in self.events, f"dup event {ev}"
                self.events[ev] = (sem, sem.count)
            else:
                assert self.events[ev] == (sem, sem.count), f"event drift {ev}"
        if self.dry:
            self.streams[role].append(("op", ev))

    def _emit(self, role, instr_fn, sem_name, n, ev):
        sem = self.sems[sem_name]
        if not self.dry and self.role == role:
            ins = instr_fn()
            if ins is not None:
                ins.then_inc(sem.h, n)
        self._note_inc(sem, n, ev, role)

    def wait(self, role, ev):
        if self.dry:
            self.streams[role].append(("wait", ev))
            return
        if self.role == role:
            sem, val = self.events[ev]
            self.eng.wait_ge(sem.h, val)

    def waits(self, role, evs):
        for e in evs:
            self.wait(role, e)

    def dma(self, role, out, in_, ev, sem_name):
        k = self._cur_dma_n[sem_name]
        self._cur_dma_n[sem_name] = k + 1
        if k > 0:
            self.wait(role, f"_d_{sem_name}_{k - 1}")
        if ev is None:
            ev = f"_d_{sem_name}_{k}"
            self._emit(role, (lambda: self.eng.dma_start(out=out, in_=in_)),
                       sem_name, 16, ev)
        else:
            self._emit(role, (lambda: self.eng.dma_start(out=out, in_=in_)),
                       sem_name, 16, ev)
            if self.dry:
                self.events[f"_d_{sem_name}_{k}"] = self.events[ev]

    def mm(self, out, lhsT, rhs, start, stop, ev=None):
        self._emit("tensor",
                   (lambda: self.eng.matmul(out, lhsT, rhs, start=start, stop=stop)),
                   "sM", 1, ev)

    def act(self, out, in_, func, ev=None, bias=0.0, scale=1.0):
        self._emit("scalar",
                   (lambda: self.eng.activation(out, in_, func, bias=bias, scale=scale)),
                   "sA", 1, ev)

    def tt(self, out, a, b2, op, ev=None):
        self._emit("vector", (lambda: self.eng.tensor_tensor(out, a, b2, op)),
                   "sV", 1, ev)

    def ts(self, out, a, s1, op, ev=None):
        self._emit("vector", (lambda: self.eng.tensor_single_scalar(out, a, s1, op)),
                   "sV", 1, ev)

    def stt(self, out, in0, scal, in1, op0, op1, ev=None):
        self._emit("vector",
                   (lambda: self.eng.scalar_tensor_tensor(out, in0, scal, in1, op0, op1)),
                   "sV", 1, ev)

    def vcopy(self, out, in_, ev=None):
        self._emit("vector", (lambda: self.eng.tensor_copy(out, in_)), "sV", 1, ev)

    def recip(self, out, in_, ev=None):
        def _f():
            if out.tensor.dtype != in_.tensor.dtype:
                with self.nc.allow_low_precision(
                        reason="recip feeds a bf16 matmul operand; rounding is "
                               "identical to the previous explicit bf16 cast"):
                    return self.eng.reciprocal(out, in_)
            return self.eng.reciprocal(out, in_)
        self._emit("vector", _f, "sV", 1, ev)

    def memset(self, ap, val, ev=None):
        self._emit("vector", (lambda: self.eng.memset(ap, val)), "sV", 1, ev)

    def cc(self, ins_, outs_, groups, ev=None):
        self._emit("gpsimd",
                   (lambda: self.eng.collective_compute(
                       "AllReduce", ALU.add, replica_groups=groups,
                       ins=ins_, outs=outs_)),
                   "sP", 1, ev)

    def check(self):
        pos = {r: 0 for r in self.ROLES}
        done = set()
        made_progress = True
        while made_progress:
            made_progress = False
            for r in self.ROLES:
                s = self.streams[r]
                while pos[r] < len(s):
                    kind, ev = s[pos[r]]
                    if kind == "wait":
                        if ev not in self.events:
                            raise AssertionError(f"wait on undefined event {ev} ({r})")
                        if self.events[ev] not in done:
                            break
                    else:
                        if ev is not None:
                            done.add(self.events[ev])
                    pos[r] += 1
                    made_progress = True
        stuck = {r: self.streams[r][pos[r]]
                 for r in self.ROLES if pos[r] < len(self.streams[r])}
        assert not stuck, f"DEADLOCK at {stuck} (pos={pos})"


def _carve(arena, off, nbytes, dtype, p0=0, p1=128):
    return arena[p0:p1, off:off + nbytes].bitcast(dtype)


def build_nc(debug=False, stop_after=None):
    nc = bass.Bass("TRN2", target_bir_lowering=False)

    xT = nc.dram_tensor("xT", [2048, NTOK], F32, kind="ExternalInput").ap()
    wqT = nc.dram_tensor("wqT", [2048, 2048], F32, kind="ExternalInput").ap()
    wkT = nc.dram_tensor("wkT", [2048, 512], F32, kind="ExternalInput").ap()
    wvT = nc.dram_tensor("wvT", [2048, 512], F32, kind="ExternalInput").ap()
    woT = nc.dram_tensor("woT", [2048, 2048], F32, kind="ExternalInput").ap()
    bq = nc.dram_tensor("bq", [1, 2048], F32, kind="ExternalInput").ap()
    bk = nc.dram_tensor("bk", [1, 512], F32, kind="ExternalInput").ap()
    bv = nc.dram_tensor("bv", [1, 512], F32, kind="ExternalInput").ap()
    wg = nc.dram_tensor("wg", [32, 128, 16, 256], I8, kind="ExternalInput").ap()
    wu = nc.dram_tensor("wu", [64, 128, 16, 128], I8, kind="ExternalInput").ap()
    wd = nc.dram_tensor("wd", [8192, 2048], I8, kind="ExternalInput").ap()
    sg = nc.dram_tensor("sg", [128, 64], F32, kind="ExternalInput").ap()
    su = nc.dram_tensor("su", [128, 64], F32, kind="ExternalInput").ap()
    sd = nc.dram_tensor("sd", [128, 16], F32, kind="ExternalInput").ap()
    out = nc.dram_tensor("out", [2048, NTOK], F32, kind="ExternalOutput").ap()

    sq_sp = nc.dram_tensor("sq_sp", [128, NT * 8192], BF16,
                           kind="ExternalOutput" if debug else "Internal").ap()
    x1_sp = nc.dram_tensor("x1_sp", [128, NT * 8192], F32,
                           kind="ExternalOutput" if debug else "Internal").ap()
    cc_in_h = nc.dram_tensor("cc_in", [128, 516], F32, kind="Internal")
    cc_out_h = nc.dram_tensor("cc_out", [128, 516], F32, kind="Internal")
    cc_dbg_h = (nc.dram_tensor("cc_dbg", [128, 516], F32, kind="ExternalOutput")
                if debug else None)

    bld = Bld()
    bld.nc = nc
    b = bld
    with ExitStack() as ctx:
        e = ctx.enter_context
        arena_h = e(nc.sbuf_tensor("arena", [128, ARENA_BYTES], U8))
        ar = arena_h.ap()
        ones_col = e(nc.sbuf_tensor("ones_col", [128, 1], BF16))
        ones128 = e(nc.sbuf_tensor("ones128", [1, 128], BF16))
        sg_sb = e(nc.sbuf_tensor("sg_sb", [128, 64], F32))
        su_sb = e(nc.sbuf_tensor("su_sb", [128, 64], F32))
        sd_sb = e(nc.sbuf_tensor("sd_sb", [128, 16], F32))
        pb = [e(nc.psum_tensor(f"pb{i}", [128, 512], F32)) for i in range(8)]
        for s in bld.sems.values():
            s.h = e(nc.semaphore(s.name))
        block = e(nc.Block())

        wq_sb = _carve(ar, O_WQ, 65536, BF16)
        wk_sb = _carve(ar, O_WK, 16384, BF16)
        wv_sb = _carve(ar, O_WV, 16384, BF16)
        wo_sb = _carve(ar, O_WO, 65536, BF16)
        sq2 = [_carve(ar, o, 16384, BF16) for o in O_SQ2]
        f_sb = _carve(ar, O_F, 65536, BF16)
        g_sb = [_carve(ar, o, 8192, BF16) for o in O_G]
        u_sb = [_carve(ar, o, 4096, BF16) for o in O_U]
        wd_sb = [_carve(ar, o, 8192, BF16) for o in O_WD]
        x1sq = [_carve(ar, o, 1024, BF16) for o in O_X1SQ]
        dsq4 = [_carve(ar, o, 1024, BF16) for o in O_DSQ4]
        rbx3 = _carve(ar, O_RBX3, 1024, BF16)
        nbx3 = _carve(ar, O_NBX3, 1024, BF16)
        xb = [_carve(ar, o, 16384, BF16) for o in O_XB]
        xn = _carve(ar, O_XN, 16384, BF16)
        sq1 = _carve(ar, O_SQ1, 16384, BF16)
        x2in = _carve(ar, O_X2IN, 32768, F32)
        tmp2 = [_carve(ar, o, 1024, BF16) for o in O_TMP2]
        x1f = _carve(ar, O_X1F, 32768, F32)
        x1b = _carve(ar, O_X1B, 16384, BF16)
        x1n = _carve(ar, O_X1N, 16384, BF16)
        sk_sb = _carve(ar, O_SK, 4096, BF16)
        vx_sb = _carve(ar, O_VX, 4128, BF16)
        kve = _carve(ar, O_KVE, 2064, F32)
        eqmn = [_carve(ar, o, 2048, F32) for o in O_EQMN]
        eqex = [_carve(ar, o, 1024, BF16) for o in O_EQEX]
        eqmx = [_carve(ar, o, 1024, BF16) for o in O_EQMX]
        apt1 = [_carve(ar, o, 1024, BF16) for o in O_APT1]
        att = _carve(ar, O_ATT, 16384, BF16)
        attb = _carve(ar, O_ATTB, 16384, BF16)
        aob = _carve(ar, O_AOB, 16384, BF16)
        ds = _carve(ar, O_DS, 16384, BF16)
        us_s = [_carve(ar, o, 1024, BF16) for o in O_US]
        fs_s = [_carve(ar, o, 1024, BF16) for o in O_FS]
        x3t = [_carve(ar, o, 1024, BF16) for o in O_X3T]
        xsq = [_carve(ar, o, 1024, BF16) for o in O_XSQ]
        rbx1 = _carve(ar, O_RBX1, 1024, BF16)
        nbx1 = _carve(ar, O_NBX1, 1024, BF16)
        aosq = [_carve(ar, o, 1024, BF16) for o in O_AOSQ]
        x1c = [_carve(ar, o, 2048, F32) for o in O_X1C]
        rb2 = _carve(ar, O_RB2, 1024, BF16)
        nb2 = _carve(ar, O_NB2, 1024, BF16)
        kv_sb = _carve(ar, O_KVSB, 1032, BF16)
        s1_sb = _carve(ar, O_S1, 2048, F32)
        ob = [_carve(ar, o, 2048, F32) for o in O_OB]
        rbd = _carve(ar, O_RBD, 1024, BF16)
        nbd = _carve(ar, O_NBD, 1024, BF16)

        def row(off, nbytes, dtype, lane=0):
            return _carve(ar, off, nbytes, dtype, p0=lane, p1=lane + 1)

        bq_bf = row(RW1_BQ, 4096, BF16)
        r1rbf = row(RW1_RBF, 1024, BF16)
        r1nbf = row(RW1_NBF, 1024, BF16)
        ones_row = row(RW1_ONE, 1024, BF16)
        r1m = row(RW1_M, 2048, F32, 32)
        r1q = row(RW1_Q, 2048, F32, 32)
        r1l = row(RW1_L, 2048, F32, 32)
        r1r = row(RW1_R, 2048, F32, 32)
        bk_bf = row(RW1_BK, 1024, BF16)
        bv_bf = row(RW1_BV, 1024, BF16)
        stg = row(RW1_STG, 4096, F32, 64)
        r2rbf = row(RW2_RBF, 1024, BF16)
        r2nbf = row(RW2_NBF, 1024, BF16)
        recbf = [row(RW2_RECBF, 1024, BF16), row(RW2_RECBF2, 1024, BF16)]
        r2m = row(RW2_M, 2048, F32, 32)
        r2q = row(RW2_Q, 2048, F32, 32)
        r2l = row(RW2_L, 2048, F32, 32)
        r2r = row(RW2_R, 2048, F32, 32)
        r2norm = row(RW2_NORM, 2048, F32, 64)
        r2rec = row(RW2_REC, 2048, F32, 64)
        r3drbf = row(RW3_DRBF, 1024, BF16)
        r3dnbf = row(RW3_DNBF, 1024, BF16)
        r3xrbf = row(RW3_XRBF, 1024, BF16)
        r3xnbf = row(RW3_XNBF, 1024, BF16)
        r3m = row(RW3_M, 2048, F32, 32)
        r3q = row(RW3_Q, 2048, F32, 32)
        r3l = row(O_RL3, 2048, F32, 32)
        r3r = row(RW3_RR, 2048, F32, 32)

        def ch(ap, c, w=512):
            return ap[:, c * w:(c + 1) * w]

        xTr = xT.rearrange("(c p) t -> p c t", p=128)

        def rows_chain(tag, ps_sum, ps_sq, ev_sum, ev_sq, rm, rq, rl, rr,
                       rbf, nbf, waits_extra=()):
            b.wait("vector", ev_sum)
            b.waits("vector", list(waits_extra))
            b.ts(rm, ps_sum[0:1, :], INV_D, ALU.mult, ev=f"{tag}_rm")
            b.wait("vector", ev_sq)
            b.ts(rq, ps_sq[0:1, :], INV_D, ALU.mult, ev=f"{tag}_rq")
            b.wait("vector", f"{tag}_rm")
            b.tt(rl, rm, rm, ALU.mult, ev=f"{tag}_rmm")
            b.wait("vector", f"{tag}_rmm")
            b.stt(rq, rq, EPS_LN, rl, ALU.add, ALU.subtract, ev=f"{tag}_rv")
            b.wait("scalar", f"{tag}_rv")
            b.act(rl, rq, AF.Ln, ev=f"{tag}_rln")
            b.wait("scalar", f"{tag}_rln")
            b.act(rr, rl, AF.Exp, scale=-0.5, ev=f"{tag}_rr")
            b.wait("vector", f"{tag}_rr")
            b.stt(rm, rm, -1.0, rr, ALU.mult, ALU.mult, ev=f"{tag}_rn")
            b.wait("scalar", f"{tag}_rr")
            b.act(rbf, rr, AF.Copy, ev=f"{tag}_rbf")
            b.wait("scalar", f"{tag}_rn")
            b.act(nbf, rm, AF.Copy, ev=f"{tag}_rows")

        rot1 = []

        def acquire1(free_evs):
            idx = len(rot1)
            bank = idx % 4
            if idx >= 4:
                for ev2 in rot1[idx - 4]:
                    b.wait("tensor", ev2)
            rot1.append(free_evs)
            return pb[bank]

        def _emit_q_group(tq, m):
            pq = acquire1([f"eqmn_{tq}_{m}", f"eqmx_{tq}_{m}"])
            b.wait("tensor", f"xn_{tq}_{NCH - 1}")
            for kc in range(NCH):
                b.mm(pq[:], wq_sb[:, kc * 2048 + m * 128: kc * 2048 + (m + 1) * 128],
                     ch(xn, kc), start=(kc == 0), stop=False)
            b.mm(pq[:], bq_bf[0:1, m * 128:(m + 1) * 128], ones_row,
                 start=False, stop=True, ev=f"q_{tq}_{m}")
            b.wait("vector", f"q_{tq}_{m}")
            if m >= 2:
                b.wait("vector", f"eqex_{tq}_{m - 2}")
            else:
                b.wait("vector", f"ekex_{tq}_{m + 2}")
            b.ts(eqmn[m % 2], pq[:], 0.0, ALU.min, ev=f"eqmn_{tq}_{m}")
            b.wait("scalar", f"eqmn_{tq}_{m}")
            if m >= 2:
                b.wait("scalar", f"sqa_{tq}_{m - 2}")
            else:
                b.wait("scalar", f"sk_{tq}_{m + 2}")
            b.act(eqex[m % 2], eqmn[m % 2][:], AF.Exp, ev=f"eqex_{tq}_{m}")
            b.wait("scalar", f"q_{tq}_{m}")
            b.act(eqmx[m % 2], pq[:], AF.Relu, ev=f"eqmx_{tq}_{m}")
            b.waits("vector", [f"eqex_{tq}_{m}", f"eqmx_{tq}_{m}"])
            if tq > 0:
                b.wait("vector", f"st_sq_{tq - 1}")
            b.tt(ch(sq1, m), eqex[m % 2][:], eqmx[m % 2][:], ALU.add,
                 ev=f"sqa_{tq}_{m}")

        def _emit_kv_phase(tq):
            b.wait("tensor", f"xn_{tq}_{NCH - 1}")
            for mu in range(4):
                pk = acquire1([f"ekmn_{tq}_{mu}", f"ekmx_{tq}_{mu}"])
                for kc in range(NCH):
                    b.mm(pk[:], xn[:, kc * 512 + mu * 128: kc * 512 + (mu + 1) * 128],
                         ch(wk_sb, kc), start=(kc == 0), stop=False)
                b.mm(pk[:], ones128[:], bk_bf[0:1, 0:512], start=False, stop=True,
                     ev=f"k_{tq}_{mu}")
                b.wait("vector", f"k_{tq}_{mu}")
                if mu < 2:
                    if tq > 0:
                        b.wait("vector", f"eqex_{tq - 1}_{14 + mu}")
                else:
                    b.wait("vector", f"ekex_{tq}_{mu - 2}")
                b.ts(eqmn[mu % 2], pk[:], 0.0, ALU.min, ev=f"ekmn_{tq}_{mu}")
                b.wait("scalar", f"ekmn_{tq}_{mu}")
                if mu < 2:
                    if tq > 0:
                        b.wait("scalar", f"sqa_{tq - 1}_{14 + mu}")
                else:
                    b.wait("scalar", f"sk_{tq}_{mu - 2}")
                b.act(eqex[mu % 2], eqmn[mu % 2][:], AF.Exp, ev=f"ekex_{tq}_{mu}")
                b.wait("scalar", f"k_{tq}_{mu}")
                b.act(eqmx[mu % 2], pk[:], AF.Relu, ev=f"ekmx_{tq}_{mu}")
                b.waits("vector", [f"ekex_{tq}_{mu}", f"ekmx_{tq}_{mu}"])
                if tq > 0:
                    b.wait("vector", f"kv_{tq - 1}_{mu}")
                b.tt(ch(sk_sb, mu), eqex[mu % 2][:], eqmx[mu % 2][:], ALU.add,
                     ev=f"sk_{tq}_{mu}")
                pv = acquire1([f"vx_{tq}_{mu}"])
                for kc in range(NCH):
                    b.mm(pv[:], xn[:, kc * 512 + mu * 128: kc * 512 + (mu + 1) * 128],
                         ch(wv_sb, kc), start=(kc == 0), stop=False)
                b.mm(pv[:], ones128[:], bv_bf[0:1, 0:512], start=False, stop=True,
                     ev=f"v_{tq}_{mu}")
                b.wait("vector", f"v_{tq}_{mu}")
                if tq > 0:
                    b.wait("vector", f"kv_{tq - 1}_{mu}")
                b.vcopy(vx_sb.rearrange("p (h c) -> p h c", c=129)
                        [:, 4 * mu:4 * (mu + 1), 0:128],
                        pv[:].rearrange("p (h c) -> p h c", c=128),
                        ev=f"vx_{tq}_{mu}")
            for mu in range(4):
                b.waits("tensor", [f"sk_{tq}_{mu}", f"vx_{tq}_{mu}"])
                for kk in range(KVH):
                    g = mu * 4 + kk
                    last = (tq == NT - 1 and mu == 3 and kk == KVH - 1)
                    b.mm(pb[4 + kk][:, 0:129],
                         sk_sb[:, mu * 512 + kk * 128: mu * 512 + (kk + 1) * 128],
                         vx_sb[:, g * 129:(g + 1) * 129],
                         start=(tq == 0 and mu == 0), stop=(tq == NT - 1 and mu == 3),
                         ev=("kv_done" if last else
                             (f"kv_{tq}_{mu}" if kk == KVH - 1 else None)))

        def _emit_p2_loads(t):
            b.wait("sync", f"st_sq_{t}")
            if t >= 2:
                b.wait("sync", f"nm_{t - 2}_{H - 1}")
            else:
                b.wait("sync", "vx_7_3")
            b.dma("sync", sq2[t % 2][:], sq_sp[:, t * 8192:(t + 1) * 8192],
                  f"sq2_{t}", f"sSq2{t % 2}")

        def _head_front(t, h):
            at_buf = att if t % 2 == 0 else attb
            kk = h // 4
            pq = pb[h % 2]
            if t == 0 and h == 0:
                b.wait("tensor", "kv_ld")
                b.wait("tensor", "sqa_7_15")
            b.wait("tensor", f"sq2_{t}")
            if h >= 2:
                b.wait("tensor", f"aevac_{t}_{h - 2}")
            elif t > 0:
                b.wait("tensor", f"aevac_{t - 1}_{h + 14}")
            b.mm(pq[:], kv_sb[:, kk * 129: kk * 129 + 128], ch(sq2[t % 2], h),
                 start=True, stop=True, ev=f"qkv_{t}_{h}")
            b.wait("scalar", f"qkv_{t}_{h}")
            if h == 0 and t >= 2:
                b.wait("scalar", f"op_{t - 2}_{NCH - 1}")
            if h == 0 and t == 1:
                b.wait("scalar", f"st_sq_{NT - 1}")
            if h == 8 and t == 0:
                b.waits("scalar", ["cc_st", "q_7_15"])
            b.act(ch(at_buf, h), pq[:], AF.Copy, ev=f"aevac_{t}_{h}")
            pn = pb[2]
            if h >= 1:
                b.wait("tensor", f"ne_{t}_{h - 1}")
            elif t > 0:
                b.wait("tensor", f"ne_{t - 1}_{H - 1}")
            b.mm(pn[0:1, :], kv_sb[:, kk * 129 + 128: kk * 129 + 129],
                 ch(sq2[t % 2], h), start=True, stop=True, ev=f"nm_{t}_{h}")
            b.wait("vector", f"nm_{t}_{h}")
            if h >= 1:
                b.wait("vector", f"lnn_{t}_{h - 1}")
            elif t > 0:
                b.wait("vector", f"lnn_{t - 1}_{H - 1}")
            b.ts(r2norm, pn[0:1, :], EPS_ATTN, ALU.add, ev=f"ne_{t}_{h}")
            b.wait("scalar", f"ne_{t}_{h}")
            if h >= 1:
                b.wait("scalar", f"rec_{t}_{h - 1}")
            elif t > 0:
                b.wait("scalar", f"rec_{t - 1}_{H - 1}")
            b.act(r2rec, r2norm, AF.Ln, ev=f"lnn_{t}_{h}")
            b.wait("scalar", f"lnn_{t}_{h}")
            if h >= 2:
                b.wait("scalar", f"bcr_{t}_{h - 2}")
            elif t > 0:
                b.wait("scalar", f"bcr_{t - 1}_{h + 14}")
            b.act(recbf[h % 2], r2rec, AF.Exp, scale=-1.0, ev=f"rec_{t}_{h}")

        def _head_back(t, h):
            at_buf = att if t % 2 == 0 else attb
            pc = pb[3]
            b.wait("tensor", f"rec_{t}_{h}")
            if h >= 1:
                b.wait("tensor", f"at_{t}_{h - 1}")
            elif t > 0:
                b.wait("tensor", f"at_{t - 1}_{H - 1}")
            b.mm(pc[:], ones128[:], recbf[h % 2], start=True, stop=True,
                 ev=f"bcr_{t}_{h}")
            b.waits("vector", [f"aevac_{t}_{h}", f"bcr_{t}_{h}"])
            if h == 0 and t >= 2:
                b.wait("vector", f"op_{t - 2}_{NCH - 1}")
            b.tt(ch(at_buf, h), ch(at_buf, h), pc[:], ALU.mult, ev=f"at_{t}_{h}")

        def _op_step(t, m):
            at_buf = att if t % 2 == 0 else attb
            po = pb[4 + m % 2]
            if m == 0:
                b.wait("tensor", f"at_{t}_{H - 1}")
                if t == 0:
                    b.waits("tensor", ["wo_ld2", "wo_ld"])
            if m >= 2:
                b.waits("tensor", [f"aoe_{t}_{m - 2}", f"aosq_{t}_{m - 2}"])
            elif t > 0:
                b.waits("tensor", [f"aoe_{t - 1}_{m + 14}",
                                   f"aosq_{t - 1}_{m + 14}"])
            for kc in range(NCH):
                b.mm(po[:], wo_sb[:, kc * 2048 + m * 128: kc * 2048 + (m + 1) * 128],
                     ch(at_buf, kc), start=(kc == 0), stop=(kc == NCH - 1),
                     ev=(f"op_{t}_{m}" if kc == NCH - 1 else None))
            b.wait("scalar", f"op_{t}_{m}")
            if t > 0:
                b.wait("scalar", f"t2a_{t - 1}_{m}")
            b.act(ch(aob, m), po[:], AF.Copy, ev=f"aoe_{t}_{m}")
            if m >= 2:
                b.wait("scalar", f"aoqm_{t}_{m - 2}")
            elif t > 0:
                b.wait("scalar", f"aoqm_{t - 1}_{m + 14}")
            b.act(aosq[m % 2], po[:], AF.Square, ev=f"aosq_{t}_{m}")

        def _op_stats(t, m):
            b.wait("tensor", f"aoe_{t}_{m}")
            if m == 0 and t > 0:
                b.wait("tensor", f"nb2e_{t - 1}")
            b.mm(pb[6][0:1, :], ones_col[:], ch(aob, m), start=(m == 0),
                 stop=(m == NCH - 1),
                 ev=(f"aos_{t}" if m == NCH - 1 else None))
            b.wait("tensor", f"aosq_{t}_{m}")
            b.mm(pb[7][0:1, :], ones_col[:], aosq[m % 2][:], start=(m == 0),
                 stop=(m == NCH - 1), ev=f"aoqm_{t}_{m}")

        def _emit_pass2_tail(t):
            rows_chain(f"p2_{t}", pb[6], pb[7], f"aos_{t}", f"aoqm_{t}_{NCH - 1}",
                       r2m, r2q, r2l, r2r, r2rbf, r2nbf)
            b.waits("tensor", [f"p2_{t}_rbf", f"p2_{t}_rm", f"p2_{t}_rq"])
            b.mm(pb[6][:], ones128[:], r2rbf, start=True, stop=True, ev=f"bcR2_{t}")
            b.wait("tensor", f"p2_{t}_rows")
            b.mm(pb[7][:], ones128[:], r2nbf, start=True, stop=True, ev=f"bcN2_{t}")
            b.wait("scalar", f"bcR2_{t}")
            b.act(rb2[:], pb[6][:], AF.Copy, ev=f"rb2e_{t}")
            b.wait("scalar", f"bcN2_{t}")
            b.act(nb2[:], pb[7][:], AF.Copy, ev=f"nb2e_{t}")

        def _apply_step(t, kc):
            if kc == 0:
                b.waits("vector", [f"rb2e_{t}", f"nb2e_{t}", f"x_{t}"])
            b.tt(tmp2[kc % 2], ch(aob, kc), rb2[:], ALU.mult, ev=f"t2a_{t}_{kc}")
            if kc >= 2:
                b.wait("vector", f"stx1_{t}_{kc - 2}")
            elif t > 0:
                b.wait("vector", f"stx1_{t - 1}_{kc + 14}")
            b.tt(x1c[kc % 2], ch(x2in, kc), nb2[:], ALU.add, ev=f"t2b_{t}_{kc}")
            b.wait("vector", f"t2b_{t}_{kc}")
            b.tt(x1c[kc % 2], tmp2[kc % 2][:], x1c[kc % 2][:], ALU.add,
                 ev=f"x1c_{t}_{kc}")
            b.wait("sync", f"x1c_{t}_{kc}")
            b.dma("sync", x1_sp[:, t * 8192 + kc * 512: t * 8192 + (kc + 1) * 512],
                  x1c[kc % 2][:], f"stx1_{t}_{kc}", f"sSx1{kc % 4}")

        def _x1_stats_chunk(t, c, idx=None):
            if idx is None:
                idx = c
            first = (idx == 0)
            last = (idx == NCH - 1)
            if first:
                b.wait("tensor", f"x1b_{t}" if t > 0 else "x1b_0a")
                if t == 0:
                    b.waits("tensor", [f"aosq_{NT - 1}_{NCH - 1}",
                                       f"nb2e_{NT - 1}"])
                elif t == 1:
                    b.wait("tensor", "nbx3e_0")
                else:
                    b.wait("tensor", f"dse_{t - 2}_12")
                    b.wait("tensor", f"dse_{t - 2}_13")
            if t == 0:
                b.wait("vector", "x1b_0a" if c >= 8 else "x1b_0b")
                if idx == 8:
                    b.wait("tensor", "x1b_0b")
            else:
                b.wait("vector", f"x1b_{t}")
            if idx >= 4:
                b.wait("vector", f"x1sqw_{t}_{idx - 4}")
            elif t > 0:
                b.wait("vector", f"x1sqw_{t - 1}_{idx + 12}")
            b.tt(x1sq[idx % 4], ch(x1b, c), ch(x1b, c), ALU.mult,
                 ev=f"x1sq_{t}_{c}")
            b.mm(pb[4][0:1, :], ones_col[:], ch(x1b, c),
                 start=first, stop=last,
                 ev=(f"x1s_{t}" if last else None))
            b.wait("tensor", f"x1sq_{t}_{c}")
            b.mm(pb[5][0:1, :], ones_col[:], x1sq[idx % 4][:],
                 start=first, stop=last, ev=f"x1sqw_{t}_{idx}")
            if last:
                if b.dry:
                    b.events[f"x1q_{t}_done"] = b.events[f"x1sqw_{t}_{idx}"]

        def _x1_ln(t):
            for c in range(NCH):
                _x1_stats_chunk(t, c)
            _x1_rows(t)

        def _x1_rows(t):
            extra = ([f"p3d_{t - 2}_rows"] if t >= 2 else
                     (["p3x_0_rows"] if t == 1 else
                      [f"stx1_{NT - 1}_{NCH - 2}", f"stx1_{NT - 1}_{NCH - 1}"]))
            rows_chain(f"p3x_{t}", pb[4], pb[5], f"x1s_{t}", f"x1q_{t}_done",
                       r3m, r3q, r3l, r3r, r3xrbf, r3xnbf, waits_extra=extra)

        def _x1_bcast(t):
            b.waits("tensor", [f"p3x_{t}_rbf", f"p3x_{t}_rm", f"p3x_{t}_rq"])
            b.mm(pb[4][:], ones128[:], r3xrbf, start=True, stop=True, ev=f"bcRx_{t}")
            b.wait("tensor", f"p3x_{t}_rows")
            b.mm(pb[5][:], ones128[:], r3xnbf, start=True, stop=True, ev=f"bcNx_{t}")
            b.wait("scalar", f"bcRx_{t}")
            b.act(rbx3[:], pb[4][:], AF.Copy, ev=f"rbx3e_{t}")
            b.wait("scalar", f"bcNx_{t}")
            b.act(nbx3[:], pb[5][:], AF.Copy, ev=f"nbx3e_{t}")

        def _x1_ln_apply(t, apply_wait=None):
            b.waits("vector", [f"rbx3e_{t}", f"nbx3e_{t}"])
            if apply_wait is not None:
                b.wait("vector", apply_wait)
            for c in range(NCH):
                b.tt(x3t[c % 2], ch(x1b, c), rbx3[:], ALU.mult, ev=f"x3a_{t}_{c}")
                b.wait("vector", f"x3a_{t}_{c}")
                b.tt(ch(x1n, c), x3t[c % 2][:], nbx3[:], ALU.add, ev=f"x1n_{t}_{c}")

        def _d_stats(t):
            for c in range(NCH):
                b.wait("tensor", f"dse_{t}_{c}")
                if c == 0:
                    b.wait("tensor", f"dse_{t}_14")
                    b.wait("tensor", f"dse_{t}_15")
                    if t > 0:
                        b.wait("tensor", f"nbde_{t - 1}")
                b.mm(pb[6][0:1, :], ones_col[:], ch(ds, c), start=(c == 0),
                     stop=(c == NCH - 1),
                     ev=(f"dst_{t}" if c == NCH - 1 else f"dss_{t}_{c}"))
                b.wait("vector", f"dse_{t}_{c}")
                if c >= 4:
                    b.wait("vector", f"dsq_mm_{t}_{c - 4}")
                elif t > 0:
                    b.wait("vector", f"dsq_mm_{t - 1}_{c + 12}")
                b.tt(dsq4[c % 4], ch(ds, c), ch(ds, c), ALU.mult,
                     ev=f"dsq_{t}_{c}")
                b.wait("tensor", f"dsq_{t}_{c}")
                b.mm(pb[7][0:1, :], ones_col[:], dsq4[c % 4][:], start=(c == 0),
                     stop=(c == NCH - 1), ev=f"dsq_mm_{t}_{c}")
            extra = [f"p3x_{t + 1}_rows"] if t + 1 < NT else []
            rows_chain(f"p3d_{t}", pb[6], pb[7], f"dst_{t}", f"dsq_mm_{t}_{NCH - 1}",
                       r3m, r3q, r3l, r3r, r3drbf, r3dnbf, waits_extra=extra)

        def _d_bcast(t):
            b.waits("tensor", [f"p3d_{t}_rbf", f"p3d_{t}_rm", f"p3d_{t}_rq"])
            b.mm(pb[6][:], ones128[:], r3drbf, start=True, stop=True, ev=f"bcRd_{t}")
            b.wait("tensor", f"p3d_{t}_rows")
            b.mm(pb[7][:], ones128[:], r3dnbf, start=True, stop=True, ev=f"bcNd_{t}")
            b.wait("scalar", f"bcRd_{t}")
            b.act(rbd[:], pb[6][:], AF.Copy, ev=f"rbde_{t}")
            b.wait("scalar", f"bcNd_{t}")
            b.act(nbd[:], pb[7][:], AF.Copy, ev=f"nbde_{t}")

        def _d_apply_chunk(t, c):
            if c == 0:
                b.waits("vector", [f"rbde_{t}", f"nbde_{t}", f"x1f_{t}"])
            b.tt(s1_sb[:], ch(ds, c), rbd[:], ALU.mult, ev=f"d3a_{t}_{c}")
            if c >= 2:
                b.wait("vector", f"st_out_{t}_{c - 2}")
            elif t > 0:
                b.wait("vector", f"st_out_{t - 1}_{c + 14}")
            b.wait("vector", f"d3a_{t}_{c}")
            b.tt(ob[c % 2], s1_sb[:], ch(x1f, c), ALU.add, ev=f"d3b_{t}_{c}")
            b.wait("vector", f"d3b_{t}_{c}")
            b.tt(ob[c % 2], ob[c % 2][:], nbd[:], ALU.add, ev=f"x2_{t}_{c}")
            b.wait("sync", f"x2_{t}_{c}")
            b.dma("sync", out[c * 128:(c + 1) * 128, t * T:(t + 1) * T],
                  ob[c % 2][:], f"st_out_{t}_{c}", f"sSo{c % 4}")

        def _emit_pass3_tile(t):
            rot3 = []

            def acquire3(free_evs):
                idx = len(rot3)
                bank = idx % 4
                if idx >= 4:
                    for ev2 in rot3[idx - 4]:
                        b.wait("tensor", ev2)
                elif t > 0:
                    b.wait("tensor", f"dse_{t - 1}_{8 + bank}")
                else:
                    b.wait("tensor", [f"at_{NT - 1}_{14}", f"at_{NT - 1}_{15}",
                                      f"ne_{NT - 1}_{H - 1}",
                                      f"at_{NT - 1}_{H - 1}"][bank])
                rot3.append(free_evs)
                return pb[bank]

            if t + 1 < NT:
                b.waits("gpsimd", [f"x1n_{t}_{NCH - 1}", f"x1sq_{t}_{NCH - 1}"])
                b.dma("gpsimd", x1b.rearrange("p (c t) -> p c t", t=T),
                      x1_sp.rearrange("p (tt ct) -> p tt ct", ct=512)
                      [:, (t + 1) * NCH:(t + 2) * NCH, :], f"x1b_{t + 1}",
                      "sGx1b")
            for j in range(32):
                if j + 2 < 32:
                    b.wait("gpsimd", f"gg_{t}_{j}_1")
                    b.dma("gpsimd", g_sb[j % 2].rearrange("p (kc i) -> p kc i", i=256),
                          wg[j + 2], f"g_ld_{t}_{j + 2}", f"sGg{j % 2}")
                for iu in (2 * j + 2, 2 * j + 3):
                    if iu < 64:
                        b.wait("gpsimd", f"uu_{t}_{(iu - 2) // 2}_{(iu - 2) % 2}")
                        b.dma("gpsimd",
                              u_sb[iu % 2].rearrange("p (kc i) -> p kc i", i=128),
                              wu[iu], f"u_ld_{t}_{iu}", f"sGu{iu % 2}" if iu % 2 else "sGu")

                for ml in range(2):
                    ic = j * 2 + ml
                    pg = acquire3([f"sil_{t}_{ic}"])
                    b.waits("tensor", [f"g_ld_{t}_{j}", f"x1n_{t}_{NCH - 1}"])
                    for kc in range(NCH):
                        b.mm(pg[:], g_sb[j % 2][:, kc * 256 + ml * 128:
                                               kc * 256 + (ml + 1) * 128],
                             ch(x1n, kc), start=(kc == 0), stop=(kc == NCH - 1),
                             ev=(f"gg_{t}_{j}_{ml}" if kc == NCH - 1 else None))
                    pu = acquire3([f"usc_{t}_{ic}"])
                    b.wait("tensor", f"u_ld_{t}_{ic}")
                    for kc in range(NCH):
                        b.mm(pu[:], u_sb[ic % 2][:, kc * 128:(kc + 1) * 128],
                             ch(x1n, kc), start=(kc == 0), stop=(kc == NCH - 1),
                             ev=(f"uu_{t}_{j}_{ml}" if kc == NCH - 1 else None))
                    b.wait("scalar", f"gg_{t}_{j}_{ml}")
                    if ic >= 2:
                        b.wait("scalar", f"f_{t}_{ic - 2}")
                    elif t > 0:
                        b.wait("scalar", f"f_{t - 1}_{ic + 62}")
                    else:
                        b.wait("scalar", f"x1c_{NT - 1}_{NCH - 1}")
                    b.act(fs_s[ic % 2][:], pg[:], AF.Silu, scale=sg_sb[:, ic:ic + 1],
                          ev=f"sil_{t}_{ic}")
                    b.wait("scalar", f"uu_{t}_{j}_{ml}")
                    b.act(us_s[ic % 2][:], pu[:], AF.Copy, scale=su_sb[:, ic:ic + 1],
                          ev=f"usc_{t}_{ic}")
                    b.waits("vector", [f"sil_{t}_{ic}", f"usc_{t}_{ic}"])
                    if ic == 0:
                        if t > 0:
                            b.wait("vector", f"dn_{t - 1}_1_7")
                        else:
                            b.wait("vector", f"op_{NT - 1}_{NCH - 1}")
                    b.tt(ch(f_sb, ic), fs_s[ic % 2][:], us_s[ic % 2][:], ALU.mult,
                         ev=f"f_{t}_{ic}")

                if j == 0 and t > 0:
                    _d_stats(t - 1)
                if j == 2 and t > 0:
                    _d_bcast(t - 1)
                if 4 <= j < 4 + NCH and t > 0:
                    _d_apply_chunk(t - 1, j - 4)
                if j == 3 and t + 1 < NT:
                    _x1_ln(t + 1)
                if j == 5 and t + 1 < NT:
                    _x1_bcast(t + 1)
                if j == 21 and t > 0:
                    b.wait("sync", f"x2_{t - 1}_{NCH - 1}")
                    b.dma("sync", x1f[:], x1_sp[:, t * 8192:(t + 1) * 8192],
                          f"x1f_{t}", "sSx1f")

            if t + 1 < NT:
                _x1_ln_apply(t + 1, apply_wait=f"uu_{t}_31_1")
            wdr = wd.rearrange("(ic p) o -> p ic o", p=128)
            for half in range(2):
                for icb in range(16):
                    par = icb % 2
                    if half == 0 and icb < 2:
                        b.wait("gpsimd", f"gg_{t}_30_1" if icb == 0
                               else f"gg_{t}_31_1")
                    else:
                        b.wait("gpsimd", f"dnb_{t}_{half}_{icb - 2}" if icb >= 2
                               else (f"dnb_{t}_0_14" if icb == 0
                                     else f"dn_{t}_0_7"))
                    b.dma("gpsimd",
                          wd_sb[par].rearrange("p (ic o) -> p ic o", o=1024),
                          wdr[:, icb * 4:(icb + 1) * 4,
                              half * 1024:(half + 1) * 1024],
                          f"wd_{t}_{half}_{icb}", f"sGwd{par}")
                    b.wait("tensor", f"wd_{t}_{half}_{icb}")
                    if half == 0 and icb == 0:
                        b.wait("tensor", f"f_{t}_63")
                        b.waits("tensor", [f"sil_{t}_62", f"usc_{t}_62",
                                           f"sil_{t}_63", f"usc_{t}_63"])
                        if t + 1 < NT:
                            b.waits("tensor", [f"rbx3e_{t + 1}", f"nbx3e_{t + 1}"])
                        if t > 0:
                            b.waits("tensor", [f"rbde_{t - 1}", f"nbde_{t - 1}"])
                    for icl in range(4):
                        ic = icb * 4 + icl
                        for o in range(8):
                            if half == 1 and ic == 0:
                                b.wait("tensor", f"dse_{t}_{o}")
                            last = (ic == 63)
                            b.mm(pb[o][:],
                                 wd_sb[par][:, icl * 1024 + o * 128:
                                            icl * 1024 + (o + 1) * 128],
                                 ch(f_sb, ic), start=(ic == 0), stop=last,
                                 ev=(f"dn_{t}_{half}_{o}" if last else
                                     (f"dnb_{t}_{half}_{icb}"
                                      if (icl == 3 and o == 7) else None)))
                for o in range(8):
                    c = half * 8 + o
                    role2 = "scalar" if o % 2 == 0 else "vector"
                    b.wait(role2, f"dn_{t}_{half}_{o}")
                    if half == 0 and o in (0, 1) and t > 0:
                        b.wait(role2, f"x2_{t - 1}_{NCH - 1}")
                        b.wait(role2, f"dsq_mm_{t - 1}_{NCH - 1}")
                    if o % 2 == 0:
                        b.act(ch(ds, c), pb[o][:], AF.Copy,
                              scale=sd_sb[:, c:c + 1], ev=f"dse_{t}_{c}")
                    else:
                        b.ts(ch(ds, c), pb[o][:], sd_sb[:, c:c + 1], ALU.mult,
                             ev=f"dse_{t}_{c}")

            if t + 1 < NT:
                b.wait("gpsimd", f"dnb_{t}_1_14")
                b.dma("gpsimd", g_sb[0].rearrange("p (kc i) -> p kc i", i=256),
                      wg[0], f"g_ld_{t + 1}_0", "sGg0")
                b.wait("gpsimd", f"dn_{t}_1_7")
                b.dma("gpsimd", g_sb[1].rearrange("p (kc i) -> p kc i", i=256),
                      wg[1], f"g_ld_{t + 1}_1", "sGg1")
                for iu in (0, 1):
                    b.wait("gpsimd", f"uu_{t}_{(62 + iu) // 2}_{(62 + iu) % 2}")
                    b.dma("gpsimd",
                          u_sb[iu % 2].rearrange("p (kc i) -> p kc i", i=128),
                          wu[iu], f"u_ld_{t + 1}_{iu}", f"sGu{iu % 2}" if iu % 2 else "sGu")

        def script():
            rot1.clear()
            b.memset(ones_col[:], 1.0)
            b.memset(ones128[:], 1.0)
            b.memset(ones_row, 1.0)
            b.memset(vx_sb[:], 1.0, ev="vext_ones")
            b.dma("sync", sg_sb[:], sg, "sg_ld", "sSpre")
            b.dma("sync", su_sb[:], su, "su_ld", "sSpre")
            b.dma("sync", sd_sb[:], sd, "sd_ld", "sSpre")
            b.dma("sync", stg, bq[0:1, 0:1024], "bqs_0", "sSpre")
            b.wait("scalar", "bqs_0")
            b.act(bq_bf[0:1, 0:1024], stg, AF.Copy, ev="bqb_0")
            b.wait("sync", "bqb_0")
            b.dma("sync", stg, bq[0:1, 1024:2048], "bqs_1", "sSpre")
            b.wait("scalar", "bqs_1")
            b.act(bq_bf[0:1, 1024:2048], stg, AF.Copy, ev="bqb_1")
            b.wait("sync", "bqb_1")
            b.dma("sync", stg[0:1, 0:512], bk, "bks", "sSpre")
            b.wait("scalar", "bks")
            b.act(bk_bf, stg[0:1, 0:512], AF.Copy, ev="bkb")
            b.wait("sync", "bkb")
            b.dma("sync", stg[0:1, 0:512], bv, "bvs", "sSpre")
            b.wait("scalar", "bvs")
            b.act(bv_bf, stg[0:1, 0:512], AF.Copy, ev="bvb")
            for i in range(4):
                src = wqT.rearrange("(kc p) n -> p kc n", p=128)[:, 4 * i:4 * (i + 1), :]
                dst = wq_sb.rearrange("p (kc n) -> p kc n", n=2048)[:, 4 * i:4 * (i + 1), :]
                b.dma("gpsimd", dst, src, f"wq_ld{i}" if i < 3 else "wq_ld", "sGpre")
            b.dma("gpsimd", wk_sb.rearrange("p (kc n) -> p kc n", n=512),
                  wkT.rearrange("(kc p) n -> p kc n", p=128), "wk_ld", "sGpre")
            b.dma("gpsimd", wv_sb.rearrange("p (kc n) -> p kc n", n=512),
                  wvT.rearrange("(kc p) n -> p kc n", p=128), "wv_ld", "sGpre")
            for t in (0, 1):
                b.dma("gpsimd", xb[t % 2].rearrange("p (c t) -> p c t", t=T),
                      xTr[:, :, t * T:(t + 1) * T], f"xb_{t}", f"sGx{t % 2}")

            b.waits("tensor", ["wv_ld", "bqb_1", "bkb", "bvb", "vext_ones"])

            for t in range(NT + 1):
                if t < NT:
                    b.wait("tensor", f"xb_{t}")
                    ps_sum = acquire1([f"p1_{t}_rm"])
                    for kc in range(NCH):
                        b.mm(ps_sum[0:1, :], ones_col[:], ch(xb[t % 2], kc),
                             start=(kc == 0), stop=(kc == NCH - 1),
                             ev=(f"p1sum_{t}" if kc == NCH - 1 else None))
                    ps_sq = acquire1([f"p1_{t}_rq"])
                    for kc in range(NCH):
                        b.wait("scalar", f"xb_{t}")
                        if kc >= 2:
                            b.wait("scalar", f"p1sq_{t}_{kc - 2}")
                        elif t > 0:
                            b.wait("scalar", f"p1sq_{t - 1}_{kc + 14}")
                        b.act(xsq[kc % 2], ch(xb[t % 2], kc), AF.Square,
                              ev=f"xsq_{t}_{kc}")
                        b.wait("tensor", f"xsq_{t}_{kc}")
                        b.mm(ps_sq[0:1, :], ones_col[:], xsq[kc % 2][:],
                             start=(kc == 0), stop=(kc == NCH - 1),
                             ev=f"p1sq_{t}_{kc}")
                    rows_chain(f"p1_{t}", ps_sum, ps_sq, f"p1sum_{t}",
                               f"p1sq_{t}_{NCH - 1}", r1m, r1q, r1l, r1r,
                               r1rbf, r1nbf)

                if t > 0:
                    _emit_kv_phase(t - 1)
                if t == NT:
                    b.wait("vector", "kv_done")
                    for kk in range(KVH):
                        b.vcopy(kve[:, kk * 129:(kk + 1) * 129],
                                pb[4 + kk][:, 0:129],
                                ev=("kvev" if kk == KVH - 1 else None))
                    b.wait("sync", "kvev")
                    b.dma("sync", cc_in_h.ap(), kve, "cc_st", "sScc")
                    b.wait("gpsimd", "cc_st")
                    b.cc([cc_in_h.ap().opt()], [cc_out_h.ap().opt()],
                         [[0, 1, 2, 3], [4, 5, 6, 7]], ev="cc_done")
                    b.wait("gpsimd", "cc_done")
                    b.dma("gpsimd", kv_sb, cc_out_h.ap(), "kv_ld", "sGkv")
                    if debug:
                        b.dma("gpsimd", cc_dbg_h.ap(), cc_out_h.ap(), "cc_dbg",
                              "sGdbg")

                if t < NT:
                    b.wait("tensor", f"p1_{t}_rbf")
                    pR = acquire1([f"rbx1e_{t}"])
                    b.mm(pR[:], ones128[:], r1rbf, start=True, stop=True,
                         ev=f"bcR1_{t}")
                    b.wait("tensor", f"p1_{t}_rows")
                    pN = acquire1([f"nbx1e_{t}"])
                    b.mm(pN[:], ones128[:], r1nbf, start=True, stop=True,
                         ev=f"bcN1_{t}")
                    b.wait("scalar", f"bcR1_{t}")
                    if t > 0:
                        b.wait("scalar", f"xn_{t - 1}_{NCH - 1}")
                    b.act(rbx1[:], pR[:], AF.Copy, ev=f"rbx1e_{t}")
                    b.wait("scalar", f"bcN1_{t}")
                    b.act(nbx1[:], pN[:], AF.Copy, ev=f"nbx1e_{t}")

                if t > 0:
                    if t == NT:
                        _emit_p2_loads(0)
                    for m in range(16):
                        _emit_q_group(t - 1, m)
                    b.wait("sync", f"sqa_{t - 1}_15")
                    b.dma("sync", sq_sp[:, (t - 1) * 8192:t * 8192], sq1[:],
                          f"st_sq_{t - 1}", "sSsq")

                if t < NT:
                    b.waits("vector", [f"rbx1e_{t}", f"nbx1e_{t}"])
                    if t > 0:
                        b.wait("vector", f"q_{t - 1}_15")
                    for kc in range(NCH):
                        b.tt(apt1[kc % 2], ch(xb[t % 2], kc), rbx1[:], ALU.mult,
                             ev=f"apm_{t}_{kc}")
                        b.wait("vector", f"apm_{t}_{kc}")
                        b.tt(ch(xn, kc), apt1[kc % 2][:], nbx1[:], ALU.add,
                             ev=f"xn_{t}_{kc}")

                if t > 0:
                    if t + 1 < NT:
                        b.waits("gpsimd", [f"xn_{t - 1}_{NCH - 1}",
                                           f"xsq_{t - 1}_{NCH - 1}"])
                        b.dma("gpsimd", xb[(t + 1) % 2].rearrange(
                            "p (c t) -> p c t", t=T),
                            xTr[:, :, (t + 1) * T:(t + 2) * T], f"xb_{t + 1}",
                            f"sGx{(t + 1) % 2}")

            if stop_after == "p1":
                b.wait("sync", f"st_sq_{NT - 1}")
                b.wait("sync", "kv_done")
                return
            b.wait("gpsimd", "q_7_15")
            for i in range(4):
                src = woT.rearrange("(kc p) n -> p kc n", p=128)[:, 4 * i:4 * (i + 1), :]
                dst = wo_sb.rearrange("p (kc n) -> p kc n", n=2048)[:, 4 * i:4 * (i + 1), :]
                b.dma("gpsimd", dst, src, f"wo_ld{i}" if i < 3 else "wo_ld", f"sGwo{i % 2}")

            if stop_after == "cc":
                b.wait("sync", "kv_ld")
                b.wait("sync", "wo_ld")
                return
            for h in range(H):
                _head_front(0, h)
                if h > 0:
                    _head_back(0, h - 1)
            _head_back(0, H - 1)
            for t in range(NT):
                if t + 1 < NT:
                    _emit_p2_loads(t + 1)
                for k in range(NCH):
                    _op_step(t, k)
                    if k > 0:
                        _op_stats(t, k - 1)
                    if t > 0:
                        _apply_step(t - 1, k)
                    if t + 1 < NT:
                        _head_front(t + 1, k)
                        if k > 0:
                            _head_back(t + 1, k - 1)
                _op_stats(t, NCH - 1)
                if t + 1 < NT:
                    _head_back(t + 1, NCH - 1)
                if t > 0:
                    b.wait("sync", f"x1c_{t - 1}_{NCH - 1}")
                else:
                    b.waits("sync", [f"xn_7_{NCH - 1}", f"xsq_7_{NCH - 1}",
                                     "sqa_7_15"])
                b.dma("sync", x2in.rearrange("p (c t) -> p c t", t=T),
                      xTr[:, :, t * T:(t + 1) * T], f"x_{t}", "sSx")
                _emit_pass2_tail(t)
            b.waits("gpsimd", [f"op_{NT - 1}_{NCH - 1}", "stx1_0_12",
                               "stx1_0_13", f"stx1_0_{NCH - 2}",
                               f"stx1_0_{NCH - 1}"])
            x1r = x1_sp.rearrange("p (tt ct) -> p tt ct", ct=512)
            b.dma("gpsimd", x1b.rearrange("p (c t) -> p c t", t=T)[:, 8:16, :],
                  x1r[:, 8:16, :], "x1b_0a", "sGx1b")
            b.waits("gpsimd", [f"bcN2_{NT - 1}", f"rec_{NT - 1}_{H - 1}"])
            b.dma("gpsimd", x1b.rearrange("p (c t) -> p c t", t=T)[:, 0:8, :],
                  x1r[:, 0:8, :], "x1b_0b", "sGx1b2")
            cs0 = list(range(8, 16)) + list(range(8))
            nidx = 0
            for kc in range(NCH):
                _apply_step(NT - 1, kc)
                if kc >= 6:
                    _x1_stats_chunk(0, cs0[nidx], idx=nidx)
                    nidx += 1
                    if kc >= 11 and nidx < NCH:
                        _x1_stats_chunk(0, cs0[nidx], idx=nidx)
                        nidx += 1
            while nidx < NCH:
                _x1_stats_chunk(0, cs0[nidx], idx=nidx)
                nidx += 1
            _x1_rows(0)
            _x1_bcast(0)

            if stop_after == "p2":
                b.wait("sync", f"stx1_{NT - 1}_{NCH - 1}")
                return
            b.waits("scalar", ["sg_ld", "su_ld", "sd_ld"])
            _x1_ln_apply(0)
            b.waits("gpsimd", [f"op_{NT - 1}_{NCH - 1}", f"nm_{NT - 1}_{H - 1}"])
            for j in (0, 1):
                b.dma("gpsimd", g_sb[j].rearrange("p (kc i) -> p kc i", i=256),
                      wg[j], f"g_ld_0_{j}", f"sGg{j}")
            for iu in (0, 1):
                b.dma("gpsimd", u_sb[iu % 2].rearrange("p (kc i) -> p kc i", i=128),
                      wu[iu], f"u_ld_0_{iu}", f"sGu{iu % 2}" if iu % 2 else "sGu")
            b.wait("sync", f"x1c_{NT - 1}_{NCH - 1}")
            b.dma("sync", x1f[:], x1_sp[:, 0:8192], "x1f_0", "sSx1f")

            for t in range(NT):
                _emit_pass3_tile(t)
            _d_stats(NT - 1)
            _d_bcast(NT - 1)
            for c in range(NCH):
                _d_apply_chunk(NT - 1, c)

            for c in range(NCH - 4, NCH):
                b.wait("sync", f"st_out_{NT - 1}_{c}")
            if debug:
                b.wait("sync", "cc_dbg")

        script()
        bld.check()
        bld.dry = False

        def mk_body(role):
            def body(eng):
                bld.begin(role, eng)
                script()
            return body

        block.sync(mk_body("sync"))
        block.gpsimd(mk_body("gpsimd"))
        block.tensor(mk_body("tensor"))
        block.scalar(mk_body("scalar"))
        block.vector(mk_body("vector"))

    return nc


def _prep_inputs(inputs):
    x = np.asarray(inputs["L0_attn_hidden_states"])
    wq = np.asarray(inputs["L0_attn_Wq"])
    wk = np.asarray(inputs["L0_attn_Wk"])
    wv = np.asarray(inputs["L0_attn_Wv"])
    wo = np.asarray(inputs["L0_attn_Wo"])
    w_gate = np.asarray(inputs["L0_ffn_w_gate"])
    w_up = np.asarray(inputs["L0_ffn_w_up"])
    w_down = np.asarray(inputs["L0_ffn_w_down"])

    common = dict(
        wqT=np.ascontiguousarray(wq.T),
        wkT=np.ascontiguousarray(wk.T),
        wvT=np.ascontiguousarray(wv.T),
        woT=np.ascontiguousarray(wo.T),
        bq=np.ascontiguousarray(np.asarray(inputs["L0_attn_bq"]).reshape(1, 2048)),
        bk=np.ascontiguousarray(np.asarray(inputs["L0_attn_bk"]).reshape(1, 512)),
        bv=np.ascontiguousarray(np.asarray(inputs["L0_attn_bv"]).reshape(1, 512)),
        wg=np.ascontiguousarray(
            w_gate.reshape(16, 128, 32, 256).transpose(2, 1, 0, 3)),
        wu=np.ascontiguousarray(
            w_up.reshape(16, 128, 64, 128).transpose(2, 1, 0, 3)),
        wd=np.ascontiguousarray(w_down),
        sg=np.ascontiguousarray(
            np.asarray(inputs["L0_ffn_s_gate"]).reshape(64, 128).T),
        su=np.ascontiguousarray(
            np.asarray(inputs["L0_ffn_s_up"]).reshape(64, 128).T),
        sd=np.ascontiguousarray(
            np.asarray(inputs["L0_ffn_s_down"]).reshape(16, 128).T),
    )
    in_maps = []
    for c in range(NCORE):
        bi = c // 4
        toff = (c % 4) * NTOK
        m = dict(common)
        m["xT"] = np.ascontiguousarray(x[bi, toff:toff + NTOK, :].T)
        in_maps.append(m)
    return in_maps


def kernel(**inputs):
    in_maps = _prep_inputs(inputs)
    nc = build_nc()
    res = run_bass_kernel_spmd(nc, in_maps, core_ids=list(range(NCORE)))
    outs = []
    for bi in range(2):
        cols = [np.asarray(res.results[bi * 4 + k]["out"]) for k in range(4)]
        outs.append(np.concatenate(cols, axis=1).T)
    return np.stack(outs, axis=0).astype(np.float32)


# version 16
# speedup vs baseline: 1.0111x; 1.0111x over previous
import numpy as np
from contextlib import ExitStack

import concourse.bass as bass
import concourse.mybir as mybir
from concourse.bass_utils import run_bass_kernel_spmd

F32 = mybir.dt.float32
BF16 = mybir.dt.bfloat16
I8 = mybir.dt.int8
U8 = mybir.dt.uint8
AF = mybir.ActivationFunctionType
ALU = mybir.AluOpType

NCORE = 8
NTOK = 4096
T = 512
NT = NTOK // T
NCH = 16
H = 16
KVH = 4
EPS_LN = 1e-5
EPS_ATTN = 1e-6
INV_D = 1.0 / 2048.0

A = 0
R2 = 98304
R3 = 163840
R4 = 196608
ARENA_BYTES = R4 + 15104

O_WQ = A + 0
O_WK = A + 65536
O_WV = A + 81920
O_WO = A + 0
O_SQ2 = (A + 65536, A + 81920)
O_F = A + 0
O_G = (A + 65536, A + 73728)
O_U = (A + 81920, A + 86016)
O_WD = (A + 65536, A + 73728)
O_X1SQ = tuple(163840 + 22528 + 1024 * i for i in range(4))
O_DSQ4 = tuple(163840 + 26624 + 1024 * i for i in range(4))
O_RBX3 = A + 92160
O_NBX3 = A + 93184
O_ROWS3 = A + 94208
O_XB = (R2 + 0, R2 + 16384)
O_XN = R2 + 32768
O_SQ1 = R2 + 49152
O_X2IN = R2 + 0
O_ROWS2 = R2 + 32768
O_TMP2 = (R4 + 11776, R4 + 12800)
O_ATTB = R2 + 43008
O_X1F = R2 + 0
O_X1B = R2 + 32768
O_X1N = R2 + 49152
O_SK = R3 + 0
O_VX = R3 + 4096
O_KVE = R3 + 8704
O_ROWS1 = R3 + 11264
O_EQMN = (R3 + 19456, R3 + 21504)
O_EQEX = (R3 + 23552, R3 + 24576)
O_EQMX = (R3 + 25600, R3 + 26624)
O_APT1 = (R3 + 27648, R3 + 28672)
O_ATT = R3 + 0
O_AOB = R3 + 16384
O_DS = R3 + 0
O_US = (R3 + 16384, R3 + 17408)
O_FS = (R3 + 18432, R3 + 19456)
O_X3T = (R3 + 20480, R3 + 21504)
O_XSQ = (R4 + 0, R4 + 1024)
O_RBX1 = R4 + 2048
O_NBX1 = R4 + 3072
O_AOSQ = (R4 + 0, R4 + 1024)
O_X1C = (R4 + 2048, R4 + 4096)
O_RB2 = R4 + 6144
O_NB2 = R4 + 7168
O_KVSB = R4 + 10240
O_S1 = R4 + 4096
O_OB = (R4 + 6144, R4 + 8192)
O_RBD = R4 + 10240
O_NBD = R4 + 11264
O_RL3 = R4 + 12288

RW1_BQ = O_ROWS1 + 0
RW1_RBF = O_ROWS1 + 4096
RW1_NBF = O_ROWS1 + 5120
RW1_ONE = O_ROWS1 + 6144
RW1_M = O_ROWS1 + 0
RW1_Q = O_ROWS1 + 2048
RW1_L = O_ROWS1 + 4096
RW1_R = O_ROWS1 + 6144
RW1_BK = O_ROWS1 + 7168
RW1_BV = R4 + 4096
RW1_STG = O_ROWS1 + 0
RW2_RBF = O_ROWS2 + 0
RW2_NBF = O_ROWS2 + 1024
RW2_RECBF = O_ROWS2 + 2048
RW2_RECBF2 = O_ROWS2 + 3072
RW2_M = O_ROWS2 + 0
RW2_Q = O_ROWS2 + 2048
RW2_L = O_ROWS2 + 4096
RW2_R = O_ROWS2 + 6144
RW2_NORM = O_ROWS2 + 0
RW2_REC = O_ROWS2 + 2048
RW3_DRBF = O_ROWS3 + 0
RW3_DNBF = O_ROWS3 + 1024
RW3_XRBF = O_ROWS3 + 2048
RW3_XNBF = O_ROWS3 + 3072
RW3_M = O_ROWS3 + 0
RW3_Q = O_ROWS3 + 2048
RW3_RR = R4 + 2048


class _Sem:
    __slots__ = ("h", "count", "name")
    def __init__(self, name):
        self.h = None
        self.count = 0
        self.name = name


class Bld:
    ROLES = ("sync", "gpsimd", "tensor", "scalar", "vector")

    DMA_SEMS = ("sGpre", "sGx0", "sGx1", "sGkv", "sGdbg", "sGwo0", "sGwo1",
                "sGx1b", "sGx1b2", "sGg0", "sGg1", "sGu", "sGwd0", "sGwd1",
                "sSpre", "sSsq", "sSq20", "sSq21", "sSx", "sSx1f",
                "sSx10", "sSx11", "sSx12", "sSx13", "sSo0", "sSo1", "sSo2", "sSo3", "sScc", "sGu1")

    def __init__(self):
        self.events = {}
        self.dry = True
        self.role = None
        self.eng = None
        self.sems = {n: _Sem(n) for n in
                     ("sM", "sA", "sV", "sP") + self.DMA_SEMS}
        self._cur_dma_n = {n: 0 for n in self.DMA_SEMS}
        self.streams = {r: [] for r in self.ROLES}

    def begin(self, role, eng):
        self.role = role
        self.eng = eng
        for s in self.sems.values():
            s.count = 0
        self._cur_dma_n = {n: 0 for n in self.DMA_SEMS}

    def _note_inc(self, sem, n, ev, role):
        sem.count += n
        if ev is not None:
            if self.dry:
                assert ev not in self.events, f"dup event {ev}"
                self.events[ev] = (sem, sem.count)
            else:
                assert self.events[ev] == (sem, sem.count), f"event drift {ev}"
        if self.dry:
            self.streams[role].append(("op", ev))

    def _emit(self, role, instr_fn, sem_name, n, ev):
        sem = self.sems[sem_name]
        if not self.dry and self.role == role:
            ins = instr_fn()
            if ins is not None:
                ins.then_inc(sem.h, n)
        self._note_inc(sem, n, ev, role)

    def wait(self, role, ev):
        if self.dry:
            self.streams[role].append(("wait", ev))
            return
        if self.role == role:
            sem, val = self.events[ev]
            self.eng.wait_ge(sem.h, val)

    def waits(self, role, evs):
        for e in evs:
            self.wait(role, e)

    def dma(self, role, out, in_, ev, sem_name):
        k = self._cur_dma_n[sem_name]
        self._cur_dma_n[sem_name] = k + 1
        if k > 0:
            self.wait(role, f"_d_{sem_name}_{k - 1}")
        if ev is None:
            ev = f"_d_{sem_name}_{k}"
            self._emit(role, (lambda: self.eng.dma_start(out=out, in_=in_)),
                       sem_name, 16, ev)
        else:
            self._emit(role, (lambda: self.eng.dma_start(out=out, in_=in_)),
                       sem_name, 16, ev)
            if self.dry:
                self.events[f"_d_{sem_name}_{k}"] = self.events[ev]

    def mm(self, out, lhsT, rhs, start, stop, ev=None):
        self._emit("tensor",
                   (lambda: self.eng.matmul(out, lhsT, rhs, start=start, stop=stop)),
                   "sM", 1, ev)

    def act(self, out, in_, func, ev=None, bias=0.0, scale=1.0):
        self._emit("scalar",
                   (lambda: self.eng.activation(out, in_, func, bias=bias, scale=scale)),
                   "sA", 1, ev)

    def tt(self, out, a, b2, op, ev=None):
        self._emit("vector", (lambda: self.eng.tensor_tensor(out, a, b2, op)),
                   "sV", 1, ev)

    def ts(self, out, a, s1, op, ev=None):
        self._emit("vector", (lambda: self.eng.tensor_single_scalar(out, a, s1, op)),
                   "sV", 1, ev)

    def stt(self, out, in0, scal, in1, op0, op1, ev=None):
        self._emit("vector",
                   (lambda: self.eng.scalar_tensor_tensor(out, in0, scal, in1, op0, op1)),
                   "sV", 1, ev)

    def vcopy(self, out, in_, ev=None):
        self._emit("vector", (lambda: self.eng.tensor_copy(out, in_)), "sV", 1, ev)

    def recip(self, out, in_, ev=None):
        def _f():
            if out.tensor.dtype != in_.tensor.dtype:
                with self.nc.allow_low_precision(
                        reason="recip feeds a bf16 matmul operand; rounding is "
                               "identical to the previous explicit bf16 cast"):
                    return self.eng.reciprocal(out, in_)
            return self.eng.reciprocal(out, in_)
        self._emit("vector", _f, "sV", 1, ev)

    def memset(self, ap, val, ev=None):
        self._emit("vector", (lambda: self.eng.memset(ap, val)), "sV", 1, ev)

    def cc(self, ins_, outs_, groups, ev=None):
        self._emit("gpsimd",
                   (lambda: self.eng.collective_compute(
                       "AllReduce", ALU.add, replica_groups=groups,
                       ins=ins_, outs=outs_)),
                   "sP", 1, ev)

    def check(self):
        pos = {r: 0 for r in self.ROLES}
        done = set()
        made_progress = True
        while made_progress:
            made_progress = False
            for r in self.ROLES:
                s = self.streams[r]
                while pos[r] < len(s):
                    kind, ev = s[pos[r]]
                    if kind == "wait":
                        if ev not in self.events:
                            raise AssertionError(f"wait on undefined event {ev} ({r})")
                        if self.events[ev] not in done:
                            break
                    else:
                        if ev is not None:
                            done.add(self.events[ev])
                    pos[r] += 1
                    made_progress = True
        stuck = {r: self.streams[r][pos[r]]
                 for r in self.ROLES if pos[r] < len(self.streams[r])}
        assert not stuck, f"DEADLOCK at {stuck} (pos={pos})"


def _carve(arena, off, nbytes, dtype, p0=0, p1=128):
    return arena[p0:p1, off:off + nbytes].bitcast(dtype)


def build_nc(debug=False, stop_after=None):
    nc = bass.Bass("TRN2", target_bir_lowering=False)

    xT = nc.dram_tensor("xT", [2048, NTOK], F32, kind="ExternalInput").ap()
    wqT = nc.dram_tensor("wqT", [2048, 2048], F32, kind="ExternalInput").ap()
    wkT = nc.dram_tensor("wkT", [2048, 512], F32, kind="ExternalInput").ap()
    wvT = nc.dram_tensor("wvT", [2048, 512], F32, kind="ExternalInput").ap()
    woT = nc.dram_tensor("woT", [2048, 2048], F32, kind="ExternalInput").ap()
    bq = nc.dram_tensor("bq", [1, 2048], F32, kind="ExternalInput").ap()
    bk = nc.dram_tensor("bk", [1, 512], F32, kind="ExternalInput").ap()
    bv = nc.dram_tensor("bv", [1, 512], F32, kind="ExternalInput").ap()
    wg = nc.dram_tensor("wg", [32, 128, 16, 256], I8, kind="ExternalInput").ap()
    wu = nc.dram_tensor("wu", [64, 128, 16, 128], I8, kind="ExternalInput").ap()
    wd = nc.dram_tensor("wd", [8192, 2048], I8, kind="ExternalInput").ap()
    sg = nc.dram_tensor("sg", [128, 64], F32, kind="ExternalInput").ap()
    su = nc.dram_tensor("su", [128, 64], F32, kind="ExternalInput").ap()
    sd = nc.dram_tensor("sd", [128, 16], F32, kind="ExternalInput").ap()
    out = nc.dram_tensor("out", [2048, NTOK], F32, kind="ExternalOutput").ap()

    sq_sp = nc.dram_tensor("sq_sp", [128, NT * 8192], BF16,
                           kind="ExternalOutput" if debug else "Internal").ap()
    x1_sp = nc.dram_tensor("x1_sp", [128, NT * 8192], F32,
                           kind="ExternalOutput" if debug else "Internal").ap()
    cc_in_h = nc.dram_tensor("cc_in", [128, 516], F32, kind="Internal")
    cc_out_h = nc.dram_tensor("cc_out", [128, 516], F32, kind="Internal")
    cc_dbg_h = (nc.dram_tensor("cc_dbg", [128, 516], F32, kind="ExternalOutput")
                if debug else None)

    bld = Bld()
    bld.nc = nc
    b = bld
    with ExitStack() as ctx:
        e = ctx.enter_context
        arena_h = e(nc.sbuf_tensor("arena", [128, ARENA_BYTES], U8))
        ar = arena_h.ap()
        ones_col = e(nc.sbuf_tensor("ones_col", [128, 1], BF16))
        ones128 = e(nc.sbuf_tensor("ones128", [1, 128], BF16))
        sg_sb = e(nc.sbuf_tensor("sg_sb", [128, 64], F32))
        su_sb = e(nc.sbuf_tensor("su_sb", [128, 64], F32))
        sd_sb = e(nc.sbuf_tensor("sd_sb", [128, 16], F32))
        pb = [e(nc.psum_tensor(f"pb{i}", [128, 512], F32)) for i in range(8)]
        for s in bld.sems.values():
            s.h = e(nc.semaphore(s.name))
        block = e(nc.Block())

        wq_sb = _carve(ar, O_WQ, 65536, BF16)
        wk_sb = _carve(ar, O_WK, 16384, BF16)
        wv_sb = _carve(ar, O_WV, 16384, BF16)
        wo_sb = _carve(ar, O_WO, 65536, BF16)
        sq2 = [_carve(ar, o, 16384, BF16) for o in O_SQ2]
        f_sb = _carve(ar, O_F, 65536, BF16)
        g_sb = [_carve(ar, o, 8192, BF16) for o in O_G]
        u_sb = [_carve(ar, o, 4096, BF16) for o in O_U]
        wd_sb = [_carve(ar, o, 8192, BF16) for o in O_WD]
        x1sq = [_carve(ar, o, 1024, BF16) for o in O_X1SQ]
        dsq4 = [_carve(ar, o, 1024, BF16) for o in O_DSQ4]
        rbx3 = _carve(ar, O_RBX3, 1024, BF16)
        nbx3 = _carve(ar, O_NBX3, 1024, BF16)
        xb = [_carve(ar, o, 16384, BF16) for o in O_XB]
        xn = _carve(ar, O_XN, 16384, BF16)
        sq1 = _carve(ar, O_SQ1, 16384, BF16)
        x2in = _carve(ar, O_X2IN, 32768, F32)
        tmp2 = [_carve(ar, o, 1024, BF16) for o in O_TMP2]
        x1f = _carve(ar, O_X1F, 32768, F32)
        x1b = _carve(ar, O_X1B, 16384, BF16)
        x1n = _carve(ar, O_X1N, 16384, BF16)
        sk_sb = _carve(ar, O_SK, 4096, BF16)
        vx_sb = _carve(ar, O_VX, 4128, BF16)
        kve = _carve(ar, O_KVE, 2064, F32)
        eqmn = [_carve(ar, o, 2048, F32) for o in O_EQMN]
        eqex = [_carve(ar, o, 1024, BF16) for o in O_EQEX]
        eqmx = [_carve(ar, o, 1024, BF16) for o in O_EQMX]
        apt1 = [_carve(ar, o, 1024, BF16) for o in O_APT1]
        att = _carve(ar, O_ATT, 16384, BF16)
        attb = _carve(ar, O_ATTB, 16384, BF16)
        aob = _carve(ar, O_AOB, 16384, BF16)
        ds = _carve(ar, O_DS, 16384, BF16)
        us_s = [_carve(ar, o, 1024, BF16) for o in O_US]
        fs_s = [_carve(ar, o, 1024, BF16) for o in O_FS]
        x3t = [_carve(ar, o, 1024, BF16) for o in O_X3T]
        xsq = [_carve(ar, o, 1024, BF16) for o in O_XSQ]
        rbx1 = _carve(ar, O_RBX1, 1024, BF16)
        nbx1 = _carve(ar, O_NBX1, 1024, BF16)
        aosq = [_carve(ar, o, 1024, BF16) for o in O_AOSQ]
        x1c = [_carve(ar, o, 2048, F32) for o in O_X1C]
        rb2 = _carve(ar, O_RB2, 1024, BF16)
        nb2 = _carve(ar, O_NB2, 1024, BF16)
        kv_sb = _carve(ar, O_KVSB, 1032, BF16)
        s1_sb = _carve(ar, O_S1, 2048, F32)
        ob = [_carve(ar, o, 2048, F32) for o in O_OB]
        rbd = _carve(ar, O_RBD, 1024, BF16)
        nbd = _carve(ar, O_NBD, 1024, BF16)

        def row(off, nbytes, dtype, lane=0):
            return _carve(ar, off, nbytes, dtype, p0=lane, p1=lane + 1)

        bq_bf = row(RW1_BQ, 4096, BF16)
        r1rbf = row(RW1_RBF, 1024, BF16)
        r1nbf = row(RW1_NBF, 1024, BF16)
        ones_row = row(RW1_ONE, 1024, BF16)
        r1m = row(RW1_M, 2048, F32, 32)
        r1q = row(RW1_Q, 2048, F32, 32)
        r1l = row(RW1_L, 2048, F32, 32)
        r1r = row(RW1_R, 2048, F32, 32)
        bk_bf = row(RW1_BK, 1024, BF16)
        bv_bf = row(RW1_BV, 1024, BF16)
        stg = row(RW1_STG, 4096, F32, 64)
        r2rbf = row(RW2_RBF, 1024, BF16)
        r2nbf = row(RW2_NBF, 1024, BF16)
        recbf = [row(RW2_RECBF, 1024, BF16), row(RW2_RECBF2, 1024, BF16)]
        r2m = row(RW2_M, 2048, F32, 32)
        r2q = row(RW2_Q, 2048, F32, 32)
        r2l = row(RW2_L, 2048, F32, 32)
        r2r = row(RW2_R, 2048, F32, 32)
        r2norm = row(RW2_NORM, 2048, F32, 64)
        r2rec = row(RW2_REC, 2048, F32, 64)
        r3drbf = row(RW3_DRBF, 1024, BF16)
        r3dnbf = row(RW3_DNBF, 1024, BF16)
        r3xrbf = row(RW3_XRBF, 1024, BF16)
        r3xnbf = row(RW3_XNBF, 1024, BF16)
        r3m = row(RW3_M, 2048, F32, 32)
        r3q = row(RW3_Q, 2048, F32, 32)
        r3l = row(O_RL3, 2048, F32, 32)
        r3r = row(RW3_RR, 2048, F32, 32)

        def ch(ap, c, w=512):
            return ap[:, c * w:(c + 1) * w]

        xTr = xT.rearrange("(c p) t -> p c t", p=128)

        def rows_chain(tag, ps_sum, ps_sq, ev_sum, ev_sq, rm, rq, rl, rr,
                       rbf, nbf, waits_extra=()):
            b.wait("vector", ev_sum)
            b.waits("vector", list(waits_extra))
            b.ts(rm, ps_sum[0:1, :], INV_D, ALU.mult, ev=f"{tag}_rm")
            b.wait("vector", ev_sq)
            b.ts(rq, ps_sq[0:1, :], INV_D, ALU.mult, ev=f"{tag}_rq")
            b.wait("vector", f"{tag}_rm")
            b.tt(rl, rm, rm, ALU.mult, ev=f"{tag}_rmm")
            b.wait("vector", f"{tag}_rmm")
            b.stt(rq, rq, EPS_LN, rl, ALU.add, ALU.subtract, ev=f"{tag}_rv")
            b.wait("scalar", f"{tag}_rv")
            b.act(rl, rq, AF.Ln, ev=f"{tag}_rln")
            b.wait("scalar", f"{tag}_rln")
            b.act(rr, rl, AF.Exp, scale=-0.5, ev=f"{tag}_rr")
            b.wait("vector", f"{tag}_rr")
            b.stt(rm, rm, -1.0, rr, ALU.mult, ALU.mult, ev=f"{tag}_rn")
            b.wait("scalar", f"{tag}_rr")
            b.act(rbf, rr, AF.Copy, ev=f"{tag}_rbf")
            b.wait("scalar", f"{tag}_rn")
            b.act(nbf, rm, AF.Copy, ev=f"{tag}_rows")

        rot1 = []

        def acquire1(free_evs):
            idx = len(rot1)
            bank = idx % 4
            if idx >= 4:
                for ev2 in rot1[idx - 4]:
                    b.wait("tensor", ev2)
            rot1.append(free_evs)
            return pb[bank]

        def _emit_q_group(tq, m):
            pq = acquire1([f"eqmn_{tq}_{m}", f"eqmx_{tq}_{m}"])
            if tq == 0 and m == 0:
                b.wait("tensor", "wq_ld")
            b.wait("tensor", f"xn_{tq}_{NCH - 1}")
            for kc in range(NCH):
                b.mm(pq[:], wq_sb[:, kc * 2048 + m * 128: kc * 2048 + (m + 1) * 128],
                     ch(xn, kc), start=(kc == 0), stop=False)
            b.mm(pq[:], bq_bf[0:1, m * 128:(m + 1) * 128], ones_row,
                 start=False, stop=True, ev=f"q_{tq}_{m}")
            b.wait("vector", f"q_{tq}_{m}")
            if m >= 2:
                b.wait("vector", f"eqex_{tq}_{m - 2}")
            else:
                b.wait("vector", f"ekex_{tq}_{m + 2}")
            b.ts(eqmn[m % 2], pq[:], 0.0, ALU.min, ev=f"eqmn_{tq}_{m}")
            b.wait("scalar", f"eqmn_{tq}_{m}")
            if m >= 2:
                b.wait("scalar", f"sqa_{tq}_{m - 2}")
            else:
                b.wait("scalar", f"sk_{tq}_{m + 2}")
            b.act(eqex[m % 2], eqmn[m % 2][:], AF.Exp, ev=f"eqex_{tq}_{m}")
            b.wait("scalar", f"q_{tq}_{m}")
            b.act(eqmx[m % 2], pq[:], AF.Relu, ev=f"eqmx_{tq}_{m}")
            b.waits("vector", [f"eqex_{tq}_{m}", f"eqmx_{tq}_{m}"])
            if tq > 0:
                b.wait("vector", f"st_sq_{tq - 1}")
            b.tt(ch(sq1, m), eqex[m % 2][:], eqmx[m % 2][:], ALU.add,
                 ev=f"sqa_{tq}_{m}")

        def _emit_kv_phase(tq):
            if tq == 0:
                b.waits("tensor", ["wk_ld", "wv_ld"])
            b.wait("tensor", f"xn_{tq}_{NCH - 1}")
            for mu in range(4):
                pk = acquire1([f"ekmn_{tq}_{mu}", f"ekmx_{tq}_{mu}"])
                for kc in range(NCH):
                    b.mm(pk[:], xn[:, kc * 512 + mu * 128: kc * 512 + (mu + 1) * 128],
                         ch(wk_sb, kc), start=(kc == 0), stop=False)
                b.mm(pk[:], ones128[:], bk_bf[0:1, 0:512], start=False, stop=True,
                     ev=f"k_{tq}_{mu}")
                b.wait("vector", f"k_{tq}_{mu}")
                if mu < 2:
                    if tq > 0:
                        b.wait("vector", f"eqex_{tq - 1}_{14 + mu}")
                else:
                    b.wait("vector", f"ekex_{tq}_{mu - 2}")
                b.ts(eqmn[mu % 2], pk[:], 0.0, ALU.min, ev=f"ekmn_{tq}_{mu}")
                b.wait("scalar", f"ekmn_{tq}_{mu}")
                if mu < 2:
                    if tq > 0:
                        b.wait("scalar", f"sqa_{tq - 1}_{14 + mu}")
                else:
                    b.wait("scalar", f"sk_{tq}_{mu - 2}")
                b.act(eqex[mu % 2], eqmn[mu % 2][:], AF.Exp, ev=f"ekex_{tq}_{mu}")
                b.wait("scalar", f"k_{tq}_{mu}")
                b.act(eqmx[mu % 2], pk[:], AF.Relu, ev=f"ekmx_{tq}_{mu}")
                b.waits("vector", [f"ekex_{tq}_{mu}", f"ekmx_{tq}_{mu}"])
                if tq > 0:
                    b.wait("vector", f"kv_{tq - 1}_{mu}")
                b.tt(ch(sk_sb, mu), eqex[mu % 2][:], eqmx[mu % 2][:], ALU.add,
                     ev=f"sk_{tq}_{mu}")
                pv = acquire1([f"vx_{tq}_{mu}"])
                for kc in range(NCH):
                    b.mm(pv[:], xn[:, kc * 512 + mu * 128: kc * 512 + (mu + 1) * 128],
                         ch(wv_sb, kc), start=(kc == 0), stop=False)
                b.mm(pv[:], ones128[:], bv_bf[0:1, 0:512], start=False, stop=True,
                     ev=f"v_{tq}_{mu}")
                b.wait("vector", f"v_{tq}_{mu}")
                if tq > 0:
                    b.wait("vector", f"kv_{tq - 1}_{mu}")
                b.vcopy(vx_sb.rearrange("p (h c) -> p h c", c=129)
                        [:, 4 * mu:4 * (mu + 1), 0:128],
                        pv[:].rearrange("p (h c) -> p h c", c=128),
                        ev=f"vx_{tq}_{mu}")
            for mu in range(4):
                b.waits("tensor", [f"sk_{tq}_{mu}", f"vx_{tq}_{mu}"])
                for kk in range(KVH):
                    g = mu * 4 + kk
                    last = (tq == NT - 1 and mu == 3 and kk == KVH - 1)
                    b.mm(pb[4 + kk][:, 0:129],
                         sk_sb[:, mu * 512 + kk * 128: mu * 512 + (kk + 1) * 128],
                         vx_sb[:, g * 129:(g + 1) * 129],
                         start=(tq == 0 and mu == 0), stop=(tq == NT - 1 and mu == 3),
                         ev=("kv_done" if last else
                             (f"kv_{tq}_{mu}" if kk == KVH - 1 else None)))

        def _emit_p2_loads(t):
            b.wait("sync", f"st_sq_{t}")
            if t >= 2:
                b.wait("sync", f"nm_{t - 2}_{H - 1}")
            else:
                b.wait("sync", "vx_7_3")
            b.dma("sync", sq2[t % 2][:], sq_sp[:, t * 8192:(t + 1) * 8192],
                  f"sq2_{t}", f"sSq2{t % 2}")

        def _head_front(t, h):
            at_buf = att if t % 2 == 0 else attb
            kk = h // 4
            pq = pb[h % 2]
            if t == 0 and h == 0:
                b.wait("tensor", "kv_ld")
                b.wait("tensor", "sqa_7_15")
            b.wait("tensor", f"sq2_{t}")
            if h >= 2:
                b.wait("tensor", f"aevac_{t}_{h - 2}")
            elif t > 0:
                b.wait("tensor", f"aevac_{t - 1}_{h + 14}")
            b.mm(pq[:], kv_sb[:, kk * 129: kk * 129 + 128], ch(sq2[t % 2], h),
                 start=True, stop=True, ev=f"qkv_{t}_{h}")
            b.wait("scalar", f"qkv_{t}_{h}")
            if h == 0 and t >= 2:
                b.wait("scalar", f"op_{t - 2}_{NCH - 1}")
            if h == 0 and t == 1:
                b.wait("scalar", f"st_sq_{NT - 1}")
            if h == 8 and t == 0:
                b.waits("scalar", ["cc_st", "q_7_15"])
            b.act(ch(at_buf, h), pq[:], AF.Copy, ev=f"aevac_{t}_{h}")
            pn = pb[2]
            if h >= 1:
                b.wait("tensor", f"ne_{t}_{h - 1}")
            elif t > 0:
                b.wait("tensor", f"ne_{t - 1}_{H - 1}")
            b.mm(pn[0:1, :], kv_sb[:, kk * 129 + 128: kk * 129 + 129],
                 ch(sq2[t % 2], h), start=True, stop=True, ev=f"nm_{t}_{h}")
            b.wait("vector", f"nm_{t}_{h}")
            if h >= 1:
                b.wait("vector", f"lnn_{t}_{h - 1}")
            elif t > 0:
                b.wait("vector", f"lnn_{t - 1}_{H - 1}")
            b.ts(r2norm, pn[0:1, :], EPS_ATTN, ALU.add, ev=f"ne_{t}_{h}")
            b.wait("scalar", f"ne_{t}_{h}")
            if h >= 1:
                b.wait("scalar", f"rec_{t}_{h - 1}")
            elif t > 0:
                b.wait("scalar", f"rec_{t - 1}_{H - 1}")
            b.act(r2rec, r2norm, AF.Ln, ev=f"lnn_{t}_{h}")
            b.wait("scalar", f"lnn_{t}_{h}")
            if h >= 2:
                b.wait("scalar", f"bcr_{t}_{h - 2}")
            elif t > 0:
                b.wait("scalar", f"bcr_{t - 1}_{h + 14}")
            b.act(recbf[h % 2], r2rec, AF.Exp, scale=-1.0, ev=f"rec_{t}_{h}")

        def _head_back(t, h):
            at_buf = att if t % 2 == 0 else attb
            pc = pb[3]
            b.wait("tensor", f"rec_{t}_{h}")
            if h >= 1:
                b.wait("tensor", f"at_{t}_{h - 1}")
            elif t > 0:
                b.wait("tensor", f"at_{t - 1}_{H - 1}")
            b.mm(pc[:], ones128[:], recbf[h % 2], start=True, stop=True,
                 ev=f"bcr_{t}_{h}")
            b.waits("vector", [f"aevac_{t}_{h}", f"bcr_{t}_{h}"])
            if h == 0 and t >= 2:
                b.wait("vector", f"op_{t - 2}_{NCH - 1}")
            b.tt(ch(at_buf, h), ch(at_buf, h), pc[:], ALU.mult, ev=f"at_{t}_{h}")

        def _op_step(t, m):
            at_buf = att if t % 2 == 0 else attb
            po = pb[4 + m % 2]
            if m == 0:
                b.wait("tensor", f"at_{t}_{H - 1}")
                if t == 0:
                    b.waits("tensor", ["wo_ld2", "wo_ld"])
            if m >= 2:
                b.waits("tensor", [f"aoe_{t}_{m - 2}", f"aosq_{t}_{m - 2}"])
            elif t > 0:
                b.waits("tensor", [f"aoe_{t - 1}_{m + 14}",
                                   f"aosq_{t - 1}_{m + 14}"])
            for kc in range(NCH):
                b.mm(po[:], wo_sb[:, kc * 2048 + m * 128: kc * 2048 + (m + 1) * 128],
                     ch(at_buf, kc), start=(kc == 0), stop=(kc == NCH - 1),
                     ev=(f"op_{t}_{m}" if kc == NCH - 1 else None))
            b.wait("scalar", f"op_{t}_{m}")
            if t > 0:
                b.wait("scalar", f"t2a_{t - 1}_{m}")
            b.act(ch(aob, m), po[:], AF.Copy, ev=f"aoe_{t}_{m}")
            if m >= 2:
                b.wait("scalar", f"aoqm_{t}_{m - 2}")
            elif t > 0:
                b.wait("scalar", f"aoqm_{t - 1}_{m + 14}")
            b.act(aosq[m % 2], po[:], AF.Square, ev=f"aosq_{t}_{m}")

        def _op_stats(t, m):
            b.wait("tensor", f"aoe_{t}_{m}")
            if m == 0 and t > 0:
                b.wait("tensor", f"nb2e_{t - 1}")
            b.mm(pb[6][0:1, :], ones_col[:], ch(aob, m), start=(m == 0),
                 stop=(m == NCH - 1),
                 ev=(f"aos_{t}" if m == NCH - 1 else None))
            b.wait("tensor", f"aosq_{t}_{m}")
            b.mm(pb[7][0:1, :], ones_col[:], aosq[m % 2][:], start=(m == 0),
                 stop=(m == NCH - 1), ev=f"aoqm_{t}_{m}")

        def _emit_pass2_tail(t):
            rows_chain(f"p2_{t}", pb[6], pb[7], f"aos_{t}", f"aoqm_{t}_{NCH - 1}",
                       r2m, r2q, r2l, r2r, r2rbf, r2nbf)
            b.waits("tensor", [f"p2_{t}_rbf", f"p2_{t}_rm", f"p2_{t}_rq"])
            b.mm(pb[6][:], ones128[:], r2rbf, start=True, stop=True, ev=f"bcR2_{t}")
            b.wait("tensor", f"p2_{t}_rows")
            b.mm(pb[7][:], ones128[:], r2nbf, start=True, stop=True, ev=f"bcN2_{t}")
            b.wait("scalar", f"bcR2_{t}")
            b.act(rb2[:], pb[6][:], AF.Copy, ev=f"rb2e_{t}")
            b.wait("scalar", f"bcN2_{t}")
            b.act(nb2[:], pb[7][:], AF.Copy, ev=f"nb2e_{t}")

        def _apply_step(t, kc):
            if kc == 0:
                b.waits("vector", [f"rb2e_{t}", f"nb2e_{t}", f"x_{t}"])
            b.tt(tmp2[kc % 2], ch(aob, kc), rb2[:], ALU.mult, ev=f"t2a_{t}_{kc}")
            if kc >= 2:
                b.wait("vector", f"stx1_{t}_{kc - 2}")
            elif t > 0:
                b.wait("vector", f"stx1_{t - 1}_{kc + 14}")
            b.tt(x1c[kc % 2], ch(x2in, kc), nb2[:], ALU.add, ev=f"t2b_{t}_{kc}")
            b.wait("vector", f"t2b_{t}_{kc}")
            b.tt(x1c[kc % 2], tmp2[kc % 2][:], x1c[kc % 2][:], ALU.add,
                 ev=f"x1c_{t}_{kc}")
            b.wait("sync", f"x1c_{t}_{kc}")
            b.dma("sync", x1_sp[:, t * 8192 + kc * 512: t * 8192 + (kc + 1) * 512],
                  x1c[kc % 2][:], f"stx1_{t}_{kc}", f"sSx1{kc % 4}")

        def _x1_stats_chunk(t, c, idx=None):
            if idx is None:
                idx = c
            first = (idx == 0)
            last = (idx == NCH - 1)
            if first:
                b.wait("tensor", f"x1b_{t}" if t > 0 else "x1b_0a")
                if t == 0:
                    b.waits("tensor", [f"aosq_{NT - 1}_{NCH - 1}",
                                       f"nb2e_{NT - 1}"])
                elif t == 1:
                    b.wait("tensor", "nbx3e_0")
                else:
                    b.wait("tensor", f"dse_{t - 2}_12")
                    b.wait("tensor", f"dse_{t - 2}_13")
            if t == 0:
                b.wait("vector", "x1b_0a" if c >= 8 else "x1b_0b")
                if idx == 8:
                    b.wait("tensor", "x1b_0b")
            else:
                b.wait("vector", f"x1b_{t}")
            if idx >= 4:
                b.wait("vector", f"x1sqw_{t}_{idx - 4}")
            elif t > 0:
                b.wait("vector", f"x1sqw_{t - 1}_{idx + 12}")
            b.tt(x1sq[idx % 4], ch(x1b, c), ch(x1b, c), ALU.mult,
                 ev=f"x1sq_{t}_{c}")
            b.mm(pb[4][0:1, :], ones_col[:], ch(x1b, c),
                 start=first, stop=last,
                 ev=(f"x1s_{t}" if last else None))
            b.wait("tensor", f"x1sq_{t}_{c}")
            b.mm(pb[5][0:1, :], ones_col[:], x1sq[idx % 4][:],
                 start=first, stop=last, ev=f"x1sqw_{t}_{idx}")
            if last:
                if b.dry:
                    b.events[f"x1q_{t}_done"] = b.events[f"x1sqw_{t}_{idx}"]

        def _x1_ln(t):
            for c in range(NCH):
                _x1_stats_chunk(t, c)
            _x1_rows(t)

        def _x1_rows(t):
            extra = ([f"p3d_{t - 2}_rows"] if t >= 2 else
                     (["p3x_0_rows"] if t == 1 else
                      [f"stx1_{NT - 1}_{NCH - 2}", f"stx1_{NT - 1}_{NCH - 1}"]))
            rows_chain(f"p3x_{t}", pb[4], pb[5], f"x1s_{t}", f"x1q_{t}_done",
                       r3m, r3q, r3l, r3r, r3xrbf, r3xnbf, waits_extra=extra)

        def _x1_bcast(t):
            b.waits("tensor", [f"p3x_{t}_rbf", f"p3x_{t}_rm", f"p3x_{t}_rq"])
            b.mm(pb[4][:], ones128[:], r3xrbf, start=True, stop=True, ev=f"bcRx_{t}")
            b.wait("tensor", f"p3x_{t}_rows")
            b.mm(pb[5][:], ones128[:], r3xnbf, start=True, stop=True, ev=f"bcNx_{t}")
            b.wait("scalar", f"bcRx_{t}")
            b.act(rbx3[:], pb[4][:], AF.Copy, ev=f"rbx3e_{t}")
            b.wait("scalar", f"bcNx_{t}")
            b.act(nbx3[:], pb[5][:], AF.Copy, ev=f"nbx3e_{t}")

        def _x1_ln_apply(t, apply_wait=None):
            b.waits("vector", [f"rbx3e_{t}", f"nbx3e_{t}"])
            if apply_wait is not None:
                b.wait("vector", apply_wait)
            for c in range(NCH):
                b.tt(x3t[c % 2], ch(x1b, c), rbx3[:], ALU.mult, ev=f"x3a_{t}_{c}")
                b.wait("vector", f"x3a_{t}_{c}")
                b.tt(ch(x1n, c), x3t[c % 2][:], nbx3[:], ALU.add, ev=f"x1n_{t}_{c}")

        def _d_stats(t):
            for c in range(NCH):
                b.wait("tensor", f"dse_{t}_{c}")
                if c == 0:
                    b.wait("tensor", f"dse_{t}_14")
                    b.wait("tensor", f"dse_{t}_15")
                    if t > 0:
                        b.wait("tensor", f"nbde_{t - 1}")
                b.mm(pb[6][0:1, :], ones_col[:], ch(ds, c), start=(c == 0),
                     stop=(c == NCH - 1),
                     ev=(f"dst_{t}" if c == NCH - 1 else f"dss_{t}_{c}"))
                b.wait("vector", f"dse_{t}_{c}")
                if c >= 4:
                    b.wait("vector", f"dsq_mm_{t}_{c - 4}")
                elif t > 0:
                    b.wait("vector", f"dsq_mm_{t - 1}_{c + 12}")
                b.tt(dsq4[c % 4], ch(ds, c), ch(ds, c), ALU.mult,
                     ev=f"dsq_{t}_{c}")
                b.wait("tensor", f"dsq_{t}_{c}")
                b.mm(pb[7][0:1, :], ones_col[:], dsq4[c % 4][:], start=(c == 0),
                     stop=(c == NCH - 1), ev=f"dsq_mm_{t}_{c}")
            extra = [f"p3x_{t + 1}_rows"] if t + 1 < NT else []
            rows_chain(f"p3d_{t}", pb[6], pb[7], f"dst_{t}", f"dsq_mm_{t}_{NCH - 1}",
                       r3m, r3q, r3l, r3r, r3drbf, r3dnbf, waits_extra=extra)

        def _d_bcast(t):
            b.waits("tensor", [f"p3d_{t}_rbf", f"p3d_{t}_rm", f"p3d_{t}_rq"])
            b.mm(pb[6][:], ones128[:], r3drbf, start=True, stop=True, ev=f"bcRd_{t}")
            b.wait("tensor", f"p3d_{t}_rows")
            b.mm(pb[7][:], ones128[:], r3dnbf, start=True, stop=True, ev=f"bcNd_{t}")
            b.wait("scalar", f"bcRd_{t}")
            b.act(rbd[:], pb[6][:], AF.Copy, ev=f"rbde_{t}")
            b.wait("scalar", f"bcNd_{t}")
            b.act(nbd[:], pb[7][:], AF.Copy, ev=f"nbde_{t}")

        def _d_apply_chunk(t, c):
            if c == 0:
                b.waits("vector", [f"rbde_{t}", f"nbde_{t}", f"x1f_{t}"])
            b.tt(s1_sb[:], ch(ds, c), rbd[:], ALU.mult, ev=f"d3a_{t}_{c}")
            if c >= 2:
                b.wait("vector", f"st_out_{t}_{c - 2}")
            elif t > 0:
                b.wait("vector", f"st_out_{t - 1}_{c + 14}")
            b.wait("vector", f"d3a_{t}_{c}")
            b.tt(ob[c % 2], s1_sb[:], ch(x1f, c), ALU.add, ev=f"d3b_{t}_{c}")
            b.wait("vector", f"d3b_{t}_{c}")
            b.tt(ob[c % 2], ob[c % 2][:], nbd[:], ALU.add, ev=f"x2_{t}_{c}")
            b.wait("sync", f"x2_{t}_{c}")
            b.dma("sync", out[c * 128:(c + 1) * 128, t * T:(t + 1) * T],
                  ob[c % 2][:], f"st_out_{t}_{c}", f"sSo{c % 4}")

        def _emit_pass3_tile(t):
            rot3 = []

            def acquire3(free_evs):
                idx = len(rot3)
                bank = idx % 4
                if idx >= 4:
                    for ev2 in rot3[idx - 4]:
                        b.wait("tensor", ev2)
                elif t > 0:
                    b.wait("tensor", f"dse_{t - 1}_{8 + bank}")
                else:
                    b.wait("tensor", [f"at_{NT - 1}_{14}", f"at_{NT - 1}_{15}",
                                      f"ne_{NT - 1}_{H - 1}",
                                      f"at_{NT - 1}_{H - 1}"][bank])
                rot3.append(free_evs)
                return pb[bank]

            if t + 1 < NT:
                b.waits("gpsimd", [f"x1n_{t}_{NCH - 1}", f"x1sq_{t}_{NCH - 1}"])
                b.dma("gpsimd", x1b.rearrange("p (c t) -> p c t", t=T),
                      x1_sp.rearrange("p (tt ct) -> p tt ct", ct=512)
                      [:, (t + 1) * NCH:(t + 2) * NCH, :], f"x1b_{t + 1}",
                      "sGx1b")
            for j in range(32):
                if j + 2 < 32:
                    b.wait("gpsimd", f"gg_{t}_{j}_1")
                    b.dma("gpsimd", g_sb[j % 2].rearrange("p (kc i) -> p kc i", i=256),
                          wg[j + 2], f"g_ld_{t}_{j + 2}", f"sGg{j % 2}")
                for iu in (2 * j + 2, 2 * j + 3):
                    if iu < 64:
                        b.wait("gpsimd", f"uu_{t}_{(iu - 2) // 2}_{(iu - 2) % 2}")
                        b.dma("gpsimd",
                              u_sb[iu % 2].rearrange("p (kc i) -> p kc i", i=128),
                              wu[iu], f"u_ld_{t}_{iu}", f"sGu{iu % 2}" if iu % 2 else "sGu")

                for ml in range(2):
                    ic = j * 2 + ml
                    pg = acquire3([f"sil_{t}_{ic}"])
                    b.waits("tensor", [f"g_ld_{t}_{j}", f"x1n_{t}_{NCH - 1}"])
                    for kc in range(NCH):
                        b.mm(pg[:], g_sb[j % 2][:, kc * 256 + ml * 128:
                                               kc * 256 + (ml + 1) * 128],
                             ch(x1n, kc), start=(kc == 0), stop=(kc == NCH - 1),
                             ev=(f"gg_{t}_{j}_{ml}" if kc == NCH - 1 else None))
                    pu = acquire3([f"usc_{t}_{ic}"])
                    b.wait("tensor", f"u_ld_{t}_{ic}")
                    for kc in range(NCH):
                        b.mm(pu[:], u_sb[ic % 2][:, kc * 128:(kc + 1) * 128],
                             ch(x1n, kc), start=(kc == 0), stop=(kc == NCH - 1),
                             ev=(f"uu_{t}_{j}_{ml}" if kc == NCH - 1 else None))
                    b.wait("scalar", f"gg_{t}_{j}_{ml}")
                    if ic >= 2:
                        b.wait("scalar", f"f_{t}_{ic - 2}")
                    elif t > 0:
                        b.wait("scalar", f"f_{t - 1}_{ic + 62}")
                    else:
                        b.wait("scalar", f"x1c_{NT - 1}_{NCH - 1}")
                    b.act(fs_s[ic % 2][:], pg[:], AF.Silu, scale=sg_sb[:, ic:ic + 1],
                          ev=f"sil_{t}_{ic}")
                    b.wait("scalar", f"uu_{t}_{j}_{ml}")
                    b.act(us_s[ic % 2][:], pu[:], AF.Copy, scale=su_sb[:, ic:ic + 1],
                          ev=f"usc_{t}_{ic}")
                    b.waits("vector", [f"sil_{t}_{ic}", f"usc_{t}_{ic}"])
                    if ic == 0:
                        if t > 0:
                            b.wait("vector", f"dn_{t - 1}_1_7")
                        else:
                            b.wait("vector", f"op_{NT - 1}_{NCH - 1}")
                    b.tt(ch(f_sb, ic), fs_s[ic % 2][:], us_s[ic % 2][:], ALU.mult,
                         ev=f"f_{t}_{ic}")

                if j == 0 and t > 0:
                    _d_stats(t - 1)
                if j == 2 and t > 0:
                    _d_bcast(t - 1)
                if 4 <= j < 4 + NCH and t > 0:
                    _d_apply_chunk(t - 1, j - 4)
                if j == 3 and t + 1 < NT:
                    _x1_ln(t + 1)
                if j == 5 and t + 1 < NT:
                    _x1_bcast(t + 1)
                if j == 21 and t > 0:
                    b.wait("sync", f"x2_{t - 1}_{NCH - 1}")
                    b.dma("sync", x1f[:], x1_sp[:, t * 8192:(t + 1) * 8192],
                          f"x1f_{t}", "sSx1f")

            if t + 1 < NT:
                _x1_ln_apply(t + 1, apply_wait=f"uu_{t}_31_1")
            wdr = wd.rearrange("(ic p) o -> p ic o", p=128)
            for half in range(2):
                for icb in range(16):
                    par = icb % 2
                    if half == 0 and icb < 2:
                        b.wait("gpsimd", f"gg_{t}_30_1" if icb == 0
                               else f"gg_{t}_31_1")
                    else:
                        b.wait("gpsimd", f"dnb_{t}_{half}_{icb - 2}" if icb >= 2
                               else (f"dnb_{t}_0_14" if icb == 0
                                     else f"dn_{t}_0_7"))
                    b.dma("gpsimd",
                          wd_sb[par].rearrange("p (ic o) -> p ic o", o=1024),
                          wdr[:, icb * 4:(icb + 1) * 4,
                              half * 1024:(half + 1) * 1024],
                          f"wd_{t}_{half}_{icb}", f"sGwd{par}")
                    b.wait("tensor", f"wd_{t}_{half}_{icb}")
                    if half == 0 and icb == 0:
                        b.wait("tensor", f"f_{t}_63")
                        b.waits("tensor", [f"sil_{t}_62", f"usc_{t}_62",
                                           f"sil_{t}_63", f"usc_{t}_63"])
                        if t + 1 < NT:
                            b.waits("tensor", [f"rbx3e_{t + 1}", f"nbx3e_{t + 1}"])
                        if t > 0:
                            b.waits("tensor", [f"rbde_{t - 1}", f"nbde_{t - 1}"])
                    for icl in range(4):
                        ic = icb * 4 + icl
                        for o in range(8):
                            if half == 1 and ic == 0:
                                b.wait("tensor", f"dse_{t}_{o}")
                            last = (ic == 63)
                            b.mm(pb[o][:],
                                 wd_sb[par][:, icl * 1024 + o * 128:
                                            icl * 1024 + (o + 1) * 128],
                                 ch(f_sb, ic), start=(ic == 0), stop=last,
                                 ev=(f"dn_{t}_{half}_{o}" if last else
                                     (f"dnb_{t}_{half}_{icb}"
                                      if (icl == 3 and o == 7) else None)))
                for o in range(8):
                    c = half * 8 + o
                    role2 = "scalar" if o % 2 == 0 else "vector"
                    b.wait(role2, f"dn_{t}_{half}_{o}")
                    if half == 0 and o in (0, 1) and t > 0:
                        b.wait(role2, f"x2_{t - 1}_{NCH - 1}")
                        b.wait(role2, f"dsq_mm_{t - 1}_{NCH - 1}")
                    if o % 2 == 0:
                        b.act(ch(ds, c), pb[o][:], AF.Copy,
                              scale=sd_sb[:, c:c + 1], ev=f"dse_{t}_{c}")
                    else:
                        b.ts(ch(ds, c), pb[o][:], sd_sb[:, c:c + 1], ALU.mult,
                             ev=f"dse_{t}_{c}")

            if t + 1 < NT:
                b.wait("gpsimd", f"dnb_{t}_1_14")
                b.dma("gpsimd", g_sb[0].rearrange("p (kc i) -> p kc i", i=256),
                      wg[0], f"g_ld_{t + 1}_0", "sGg0")
                b.wait("gpsimd", f"dn_{t}_1_7")
                b.dma("gpsimd", g_sb[1].rearrange("p (kc i) -> p kc i", i=256),
                      wg[1], f"g_ld_{t + 1}_1", "sGg1")
                for iu in (0, 1):
                    b.wait("gpsimd", f"uu_{t}_{(62 + iu) // 2}_{(62 + iu) % 2}")
                    b.dma("gpsimd",
                          u_sb[iu % 2].rearrange("p (kc i) -> p kc i", i=128),
                          wu[iu], f"u_ld_{t + 1}_{iu}", f"sGu{iu % 2}" if iu % 2 else "sGu")

        def script():
            rot1.clear()
            b.memset(ones_col[:], 1.0)
            b.memset(ones128[:], 1.0)
            b.memset(ones_row, 1.0)
            b.memset(vx_sb[:], 1.0, ev="vext_ones")
            b.dma("sync", sg_sb[:], sg, "sg_ld", "sSpre")
            b.dma("sync", su_sb[:], su, "su_ld", "sSpre")
            b.dma("sync", sd_sb[:], sd, "sd_ld", "sSpre")
            b.dma("sync", stg, bq[0:1, 0:1024], "bqs_0", "sSpre")
            b.wait("scalar", "bqs_0")
            b.act(bq_bf[0:1, 0:1024], stg, AF.Copy, ev="bqb_0")
            b.wait("sync", "bqb_0")
            b.dma("sync", stg, bq[0:1, 1024:2048], "bqs_1", "sSpre")
            b.wait("scalar", "bqs_1")
            b.act(bq_bf[0:1, 1024:2048], stg, AF.Copy, ev="bqb_1")
            b.wait("sync", "bqb_1")
            b.dma("sync", stg[0:1, 0:512], bk, "bks", "sSpre")
            b.wait("scalar", "bks")
            b.act(bk_bf, stg[0:1, 0:512], AF.Copy, ev="bkb")
            b.wait("sync", "bkb")
            b.dma("sync", stg[0:1, 0:512], bv, "bvs", "sSpre")
            b.wait("scalar", "bvs")
            b.act(bv_bf, stg[0:1, 0:512], AF.Copy, ev="bvb")
            for t in (0, 1):
                b.dma("gpsimd", xb[t % 2].rearrange("p (c t) -> p c t", t=T),
                      xTr[:, :, t * T:(t + 1) * T], f"xb_{t}", f"sGx{t % 2}")
            b.dma("gpsimd", wk_sb.rearrange("p (kc n) -> p kc n", n=512),
                  wkT.rearrange("(kc p) n -> p kc n", p=128), "wk_ld", "sGpre")
            b.dma("gpsimd", wv_sb.rearrange("p (kc n) -> p kc n", n=512),
                  wvT.rearrange("(kc p) n -> p kc n", p=128), "wv_ld", "sGpre")
            for i in range(4):
                src = wqT.rearrange("(kc p) n -> p kc n", p=128)[:, 4 * i:4 * (i + 1), :]
                dst = wq_sb.rearrange("p (kc n) -> p kc n", n=2048)[:, 4 * i:4 * (i + 1), :]
                b.dma("gpsimd", dst, src, f"wq_ld{i}" if i < 3 else "wq_ld", "sGpre")

            b.waits("tensor", ["bqb_1", "bkb", "bvb", "vext_ones"])

            for t in range(NT + 1):
                if t < NT:
                    b.wait("tensor", f"xb_{t}")
                    ps_sum = acquire1([f"p1_{t}_rm"])
                    for kc in range(NCH):
                        b.mm(ps_sum[0:1, :], ones_col[:], ch(xb[t % 2], kc),
                             start=(kc == 0), stop=(kc == NCH - 1),
                             ev=(f"p1sum_{t}" if kc == NCH - 1 else None))
                    ps_sq = acquire1([f"p1_{t}_rq"])
                    for kc in range(NCH):
                        b.wait("scalar", f"xb_{t}")
                        if kc >= 2:
                            b.wait("scalar", f"p1sq_{t}_{kc - 2}")
                        elif t > 0:
                            b.wait("scalar", f"p1sq_{t - 1}_{kc + 14}")
                        b.act(xsq[kc % 2], ch(xb[t % 2], kc), AF.Square,
                              ev=f"xsq_{t}_{kc}")
                        b.wait("tensor", f"xsq_{t}_{kc}")
                        b.mm(ps_sq[0:1, :], ones_col[:], xsq[kc % 2][:],
                             start=(kc == 0), stop=(kc == NCH - 1),
                             ev=f"p1sq_{t}_{kc}")
                    rows_chain(f"p1_{t}", ps_sum, ps_sq, f"p1sum_{t}",
                               f"p1sq_{t}_{NCH - 1}", r1m, r1q, r1l, r1r,
                               r1rbf, r1nbf)

                if t > 0:
                    _emit_kv_phase(t - 1)
                if t == NT:
                    b.wait("vector", "kv_done")
                    for kk in range(KVH):
                        b.vcopy(kve[:, kk * 129:(kk + 1) * 129],
                                pb[4 + kk][:, 0:129],
                                ev=("kvev" if kk == KVH - 1 else None))
                    b.wait("sync", "kvev")
                    b.dma("sync", cc_in_h.ap(), kve, "cc_st", "sScc")
                    b.wait("gpsimd", "cc_st")
                    b.cc([cc_in_h.ap().opt()], [cc_out_h.ap().opt()],
                         [[0, 1, 2, 3], [4, 5, 6, 7]], ev="cc_done")
                    b.wait("gpsimd", "cc_done")
                    b.dma("gpsimd", kv_sb, cc_out_h.ap(), "kv_ld", "sGkv")
                    if debug:
                        b.dma("gpsimd", cc_dbg_h.ap(), cc_out_h.ap(), "cc_dbg",
                              "sGdbg")

                if t < NT:
                    b.wait("tensor", f"p1_{t}_rbf")
                    pR = acquire1([f"rbx1e_{t}"])
                    b.mm(pR[:], ones128[:], r1rbf, start=True, stop=True,
                         ev=f"bcR1_{t}")
                    b.wait("tensor", f"p1_{t}_rows")
                    pN = acquire1([f"nbx1e_{t}"])
                    b.mm(pN[:], ones128[:], r1nbf, start=True, stop=True,
                         ev=f"bcN1_{t}")
                    b.wait("scalar", f"bcR1_{t}")
                    if t > 0:
                        b.wait("scalar", f"xn_{t - 1}_{NCH - 1}")
                    b.act(rbx1[:], pR[:], AF.Copy, ev=f"rbx1e_{t}")
                    b.wait("scalar", f"bcN1_{t}")
                    b.act(nbx1[:], pN[:], AF.Copy, ev=f"nbx1e_{t}")

                if t > 0:
                    if t == NT:
                        _emit_p2_loads(0)
                    for m in range(16):
                        _emit_q_group(t - 1, m)
                    b.wait("sync", f"sqa_{t - 1}_15")
                    b.dma("sync", sq_sp[:, (t - 1) * 8192:t * 8192], sq1[:],
                          f"st_sq_{t - 1}", "sSsq")

                if t < NT:
                    b.waits("vector", [f"rbx1e_{t}", f"nbx1e_{t}"])
                    if t > 0:
                        b.wait("vector", f"q_{t - 1}_15")
                    for kc in range(NCH):
                        b.tt(apt1[kc % 2], ch(xb[t % 2], kc), rbx1[:], ALU.mult,
                             ev=f"apm_{t}_{kc}")
                        b.wait("vector", f"apm_{t}_{kc}")
                        b.tt(ch(xn, kc), apt1[kc % 2][:], nbx1[:], ALU.add,
                             ev=f"xn_{t}_{kc}")

                if t > 0:
                    if t + 1 < NT:
                        b.waits("gpsimd", [f"xn_{t - 1}_{NCH - 1}",
                                           f"xsq_{t - 1}_{NCH - 1}"])
                        b.dma("gpsimd", xb[(t + 1) % 2].rearrange(
                            "p (c t) -> p c t", t=T),
                            xTr[:, :, (t + 1) * T:(t + 2) * T], f"xb_{t + 1}",
                            f"sGx{(t + 1) % 2}")

            if stop_after == "p1":
                b.wait("sync", f"st_sq_{NT - 1}")
                b.wait("sync", "kv_done")
                return
            b.wait("gpsimd", "q_7_15")
            for i in range(4):
                src = woT.rearrange("(kc p) n -> p kc n", p=128)[:, 4 * i:4 * (i + 1), :]
                dst = wo_sb.rearrange("p (kc n) -> p kc n", n=2048)[:, 4 * i:4 * (i + 1), :]
                b.dma("gpsimd", dst, src, f"wo_ld{i}" if i < 3 else "wo_ld", f"sGwo{i % 2}")

            if stop_after == "cc":
                b.wait("sync", "kv_ld")
                b.wait("sync", "wo_ld")
                return
            for h in range(H):
                _head_front(0, h)
                if h > 0:
                    _head_back(0, h - 1)
            _head_back(0, H - 1)
            for t in range(NT):
                if t + 1 < NT:
                    _emit_p2_loads(t + 1)
                for k in range(NCH):
                    _op_step(t, k)
                    if k > 0:
                        _op_stats(t, k - 1)
                    if t > 0:
                        _apply_step(t - 1, k)
                    if t + 1 < NT:
                        _head_front(t + 1, k)
                        if k > 0:
                            _head_back(t + 1, k - 1)
                _op_stats(t, NCH - 1)
                if t + 1 < NT:
                    _head_back(t + 1, NCH - 1)
                if t > 0:
                    b.wait("sync", f"x1c_{t - 1}_{NCH - 1}")
                else:
                    b.waits("sync", [f"xn_7_{NCH - 1}", f"xsq_7_{NCH - 1}",
                                     "sqa_7_15"])
                b.dma("sync", x2in.rearrange("p (c t) -> p c t", t=T),
                      xTr[:, :, t * T:(t + 1) * T], f"x_{t}", "sSx")
                _emit_pass2_tail(t)
            b.waits("gpsimd", [f"op_{NT - 1}_{NCH - 1}", "stx1_0_12",
                               "stx1_0_13", f"stx1_0_{NCH - 2}",
                               f"stx1_0_{NCH - 1}"])
            x1r = x1_sp.rearrange("p (tt ct) -> p tt ct", ct=512)
            b.dma("gpsimd", x1b.rearrange("p (c t) -> p c t", t=T)[:, 8:16, :],
                  x1r[:, 8:16, :], "x1b_0a", "sGx1b")
            b.waits("gpsimd", [f"bcN2_{NT - 1}", f"rec_{NT - 1}_{H - 1}"])
            b.dma("gpsimd", x1b.rearrange("p (c t) -> p c t", t=T)[:, 0:8, :],
                  x1r[:, 0:8, :], "x1b_0b", "sGx1b2")
            cs0 = list(range(8, 16)) + list(range(8))
            for kc in range(NCH):
                _apply_step(NT - 1, kc)
                if kc >= 6:
                    _x1_stats_chunk(0, cs0[kc - 6], idx=kc - 6)
            for idx in range(10, NCH):
                _x1_stats_chunk(0, cs0[idx], idx=idx)
            _x1_rows(0)
            _x1_bcast(0)

            if stop_after == "p2":
                b.wait("sync", f"stx1_{NT - 1}_{NCH - 1}")
                return
            b.waits("scalar", ["sg_ld", "su_ld", "sd_ld"])
            _x1_ln_apply(0)
            b.waits("gpsimd", [f"op_{NT - 1}_{NCH - 1}", f"nm_{NT - 1}_{H - 1}"])
            for j in (0, 1):
                b.dma("gpsimd", g_sb[j].rearrange("p (kc i) -> p kc i", i=256),
                      wg[j], f"g_ld_0_{j}", f"sGg{j}")
            for iu in (0, 1):
                b.dma("gpsimd", u_sb[iu % 2].rearrange("p (kc i) -> p kc i", i=128),
                      wu[iu], f"u_ld_0_{iu}", f"sGu{iu % 2}" if iu % 2 else "sGu")
            b.wait("sync", f"x1c_{NT - 1}_{NCH - 1}")
            b.dma("sync", x1f[:], x1_sp[:, 0:8192], "x1f_0", "sSx1f")

            for t in range(NT):
                _emit_pass3_tile(t)
            _d_stats(NT - 1)
            _d_bcast(NT - 1)
            for c in range(NCH):
                _d_apply_chunk(NT - 1, c)

            for c in range(NCH - 4, NCH):
                b.wait("sync", f"st_out_{NT - 1}_{c}")
            if debug:
                b.wait("sync", "cc_dbg")

        script()
        bld.check()
        bld.dry = False

        def mk_body(role):
            def body(eng):
                bld.begin(role, eng)
                script()
            return body

        block.sync(mk_body("sync"))
        block.gpsimd(mk_body("gpsimd"))
        block.tensor(mk_body("tensor"))
        block.scalar(mk_body("scalar"))
        block.vector(mk_body("vector"))

    return nc


def _prep_inputs(inputs):
    x = np.asarray(inputs["L0_attn_hidden_states"])
    wq = np.asarray(inputs["L0_attn_Wq"])
    wk = np.asarray(inputs["L0_attn_Wk"])
    wv = np.asarray(inputs["L0_attn_Wv"])
    wo = np.asarray(inputs["L0_attn_Wo"])
    w_gate = np.asarray(inputs["L0_ffn_w_gate"])
    w_up = np.asarray(inputs["L0_ffn_w_up"])
    w_down = np.asarray(inputs["L0_ffn_w_down"])

    common = dict(
        wqT=np.ascontiguousarray(wq.T),
        wkT=np.ascontiguousarray(wk.T),
        wvT=np.ascontiguousarray(wv.T),
        woT=np.ascontiguousarray(wo.T),
        bq=np.ascontiguousarray(np.asarray(inputs["L0_attn_bq"]).reshape(1, 2048)),
        bk=np.ascontiguousarray(np.asarray(inputs["L0_attn_bk"]).reshape(1, 512)),
        bv=np.ascontiguousarray(np.asarray(inputs["L0_attn_bv"]).reshape(1, 512)),
        wg=np.ascontiguousarray(
            w_gate.reshape(16, 128, 32, 256).transpose(2, 1, 0, 3)),
        wu=np.ascontiguousarray(
            w_up.reshape(16, 128, 64, 128).transpose(2, 1, 0, 3)),
        wd=np.ascontiguousarray(w_down),
        sg=np.ascontiguousarray(
            np.asarray(inputs["L0_ffn_s_gate"]).reshape(64, 128).T),
        su=np.ascontiguousarray(
            np.asarray(inputs["L0_ffn_s_up"]).reshape(64, 128).T),
        sd=np.ascontiguousarray(
            np.asarray(inputs["L0_ffn_s_down"]).reshape(16, 128).T),
    )
    in_maps = []
    for c in range(NCORE):
        bi = c // 4
        toff = (c % 4) * NTOK
        m = dict(common)
        m["xT"] = np.ascontiguousarray(x[bi, toff:toff + NTOK, :].T)
        in_maps.append(m)
    return in_maps


def kernel(**inputs):
    in_maps = _prep_inputs(inputs)
    nc = build_nc()
    res = run_bass_kernel_spmd(nc, in_maps, core_ids=list(range(NCORE)))
    outs = []
    for bi in range(2):
        cols = [np.asarray(res.results[bi * 4 + k]["out"]) for k in range(4)]
        outs.append(np.concatenate(cols, axis=1).T)
    return np.stack(outs, axis=0).astype(np.float32)


# version 18
# speedup vs baseline: 1.0157x; 1.0045x over previous
import numpy as np
from contextlib import ExitStack

import concourse.bass as bass
import concourse.mybir as mybir
from concourse.bass_utils import run_bass_kernel_spmd

F32 = mybir.dt.float32
BF16 = mybir.dt.bfloat16
I8 = mybir.dt.int8
U8 = mybir.dt.uint8
AF = mybir.ActivationFunctionType
ALU = mybir.AluOpType

NCORE = 8
NTOK = 4096
T = 512
NT = NTOK // T
NCH = 16
H = 16
KVH = 4
EPS_LN = 1e-5
EPS_ATTN = 1e-6
INV_D = 1.0 / 2048.0

A = 0
R2 = 98304
R3 = 163840
R4 = 196608
ARENA_BYTES = R4 + 15104

O_WQ = A + 0
O_WK = A + 65536
O_WV = A + 81920
O_WO = A + 0
O_SQ2 = (A + 65536, A + 81920)
O_F = A + 0
O_G = (A + 65536, A + 73728)
O_U = (A + 81920, A + 86016)
O_WD = (A + 65536, A + 73728)
O_X1SQ = tuple(163840 + 22528 + 1024 * i for i in range(4))
O_DSQ4 = tuple(163840 + 26624 + 1024 * i for i in range(4))
O_RBX3 = A + 92160
O_NBX3 = A + 93184
O_ROWS3 = A + 94208
O_XB = (R2 + 0, R2 + 16384)
O_XN = R2 + 32768
O_SQ1 = R2 + 49152
O_X2IN = R2 + 0
O_ROWS2 = R2 + 32768
O_TMP2 = (R4 + 11776, R4 + 12800)
O_ATTB = R2 + 43008
O_X1F = R2 + 0
O_X1B = R2 + 32768
O_X1N = R2 + 49152
O_SK = R3 + 0
O_VX = R3 + 4096
O_KVE = R3 + 8704
O_ROWS1 = R3 + 11264
O_EQMN = (R3 + 19456, R3 + 21504)
O_EQEX = (R3 + 23552, R3 + 24576)
O_EQMX = (R3 + 25600, R3 + 26624)
O_APT1 = (R3 + 27648, R3 + 28672)
O_ATT = R3 + 0
O_AOB = R3 + 16384
O_DS = R3 + 0
O_US = (R3 + 16384, R3 + 17408)
O_FS = (R3 + 18432, R3 + 19456)
O_X3T = (R3 + 20480, R3 + 21504)
O_XSQ = (R4 + 0, R4 + 1024)
O_RBX1 = R4 + 2048
O_NBX1 = R4 + 3072
O_AOSQ = (R4 + 0, R4 + 1024)
O_X1C = (R4 + 2048, R4 + 4096)
O_RB2 = R4 + 6144
O_NB2 = R4 + 7168
O_KVSB = R4 + 10240
O_S1 = R4 + 4096
O_OB = (R4 + 6144, R4 + 8192)
O_RBD = R4 + 10240
O_NBD = R4 + 11264
O_AOB0B = R4 + 13824
O_RL3 = R4 + 12288

RW1_BQ = O_ROWS1 + 0
RW1_RBF = O_ROWS1 + 4096
RW1_NBF = O_ROWS1 + 5120
RW1_ONE = O_ROWS1 + 6144
RW1_M = O_ROWS1 + 0
RW1_Q = O_ROWS1 + 2048
RW1_L = O_ROWS1 + 4096
RW1_R = O_ROWS1 + 6144
RW1_BK = O_ROWS1 + 7168
RW1_BV = R4 + 4096
RW1_STG = O_ROWS1 + 0
RW2_RBF = O_ROWS2 + 0
RW2_NBF = O_ROWS2 + 1024
RW2_RECBF = O_ROWS2 + 2048
RW2_RECBF2 = O_ROWS2 + 3072
RW2_M = O_ROWS2 + 0
RW2_Q = O_ROWS2 + 2048
RW2_L = O_ROWS2 + 4096
RW2_R = O_ROWS2 + 6144
RW2_NORM = O_ROWS2 + 0
RW2_REC = O_ROWS2 + 2048
RW3_DRBF = O_ROWS3 + 0
RW3_DNBF = O_ROWS3 + 1024
RW3_XRBF = O_ROWS3 + 2048
RW3_XNBF = O_ROWS3 + 3072
RW3_M = O_ROWS3 + 0
RW3_Q = O_ROWS3 + 2048
RW3_RR = R4 + 2048


class _Sem:
    __slots__ = ("h", "count", "name")
    def __init__(self, name):
        self.h = None
        self.count = 0
        self.name = name


class Bld:
    ROLES = ("sync", "gpsimd", "tensor", "scalar", "vector")

    DMA_SEMS = ("sGpre", "sGx0", "sGx1", "sGkv", "sGdbg", "sGwo0", "sGwo1",
                "sGx1b", "sGx1b2", "sGg0", "sGg1", "sGu", "sGwd0", "sGwd1",
                "sSpre", "sSsq", "sSq20", "sSq21", "sSx", "sSx1f",
                "sSx10", "sSx11", "sSx12", "sSx13", "sSo0", "sSo1", "sSo2", "sSo3", "sScc", "sGu1")

    def __init__(self):
        self.events = {}
        self.dry = True
        self.role = None
        self.eng = None
        self.sems = {n: _Sem(n) for n in
                     ("sM", "sA", "sV", "sP") + self.DMA_SEMS}
        self._cur_dma_n = {n: 0 for n in self.DMA_SEMS}
        self.streams = {r: [] for r in self.ROLES}

    def begin(self, role, eng):
        self.role = role
        self.eng = eng
        for s in self.sems.values():
            s.count = 0
        self._cur_dma_n = {n: 0 for n in self.DMA_SEMS}

    def _note_inc(self, sem, n, ev, role):
        sem.count += n
        if ev is not None:
            if self.dry:
                assert ev not in self.events, f"dup event {ev}"
                self.events[ev] = (sem, sem.count)
            else:
                assert self.events[ev] == (sem, sem.count), f"event drift {ev}"
        if self.dry:
            self.streams[role].append(("op", ev))

    def _emit(self, role, instr_fn, sem_name, n, ev):
        sem = self.sems[sem_name]
        if not self.dry and self.role == role:
            ins = instr_fn()
            if ins is not None:
                ins.then_inc(sem.h, n)
        self._note_inc(sem, n, ev, role)

    def wait(self, role, ev):
        if self.dry:
            self.streams[role].append(("wait", ev))
            return
        if self.role == role:
            sem, val = self.events[ev]
            self.eng.wait_ge(sem.h, val)

    def waits(self, role, evs):
        for e in evs:
            self.wait(role, e)

    def dma(self, role, out, in_, ev, sem_name):
        k = self._cur_dma_n[sem_name]
        self._cur_dma_n[sem_name] = k + 1
        if k > 0:
            self.wait(role, f"_d_{sem_name}_{k - 1}")
        if ev is None:
            ev = f"_d_{sem_name}_{k}"
            self._emit(role, (lambda: self.eng.dma_start(out=out, in_=in_)),
                       sem_name, 16, ev)
        else:
            self._emit(role, (lambda: self.eng.dma_start(out=out, in_=in_)),
                       sem_name, 16, ev)
            if self.dry:
                self.events[f"_d_{sem_name}_{k}"] = self.events[ev]

    def mm(self, out, lhsT, rhs, start, stop, ev=None):
        self._emit("tensor",
                   (lambda: self.eng.matmul(out, lhsT, rhs, start=start, stop=stop)),
                   "sM", 1, ev)

    def act(self, out, in_, func, ev=None, bias=0.0, scale=1.0):
        self._emit("scalar",
                   (lambda: self.eng.activation(out, in_, func, bias=bias, scale=scale)),
                   "sA", 1, ev)

    def tt(self, out, a, b2, op, ev=None):
        self._emit("vector", (lambda: self.eng.tensor_tensor(out, a, b2, op)),
                   "sV", 1, ev)

    def ts(self, out, a, s1, op, ev=None):
        self._emit("vector", (lambda: self.eng.tensor_single_scalar(out, a, s1, op)),
                   "sV", 1, ev)

    def stt(self, out, in0, scal, in1, op0, op1, ev=None):
        self._emit("vector",
                   (lambda: self.eng.scalar_tensor_tensor(out, in0, scal, in1, op0, op1)),
                   "sV", 1, ev)

    def vcopy(self, out, in_, ev=None):
        self._emit("vector", (lambda: self.eng.tensor_copy(out, in_)), "sV", 1, ev)

    def recip(self, out, in_, ev=None):
        def _f():
            if out.tensor.dtype != in_.tensor.dtype:
                with self.nc.allow_low_precision(
                        reason="recip feeds a bf16 matmul operand; rounding is "
                               "identical to the previous explicit bf16 cast"):
                    return self.eng.reciprocal(out, in_)
            return self.eng.reciprocal(out, in_)
        self._emit("vector", _f, "sV", 1, ev)

    def memset(self, ap, val, ev=None):
        self._emit("vector", (lambda: self.eng.memset(ap, val)), "sV", 1, ev)

    def cc(self, ins_, outs_, groups, ev=None):
        self._emit("gpsimd",
                   (lambda: self.eng.collective_compute(
                       "AllReduce", ALU.add, replica_groups=groups,
                       ins=ins_, outs=outs_)),
                   "sP", 1, ev)

    def check(self):
        pos = {r: 0 for r in self.ROLES}
        done = set()
        made_progress = True
        while made_progress:
            made_progress = False
            for r in self.ROLES:
                s = self.streams[r]
                while pos[r] < len(s):
                    kind, ev = s[pos[r]]
                    if kind == "wait":
                        if ev not in self.events:
                            raise AssertionError(f"wait on undefined event {ev} ({r})")
                        if self.events[ev] not in done:
                            break
                    else:
                        if ev is not None:
                            done.add(self.events[ev])
                    pos[r] += 1
                    made_progress = True
        stuck = {r: self.streams[r][pos[r]]
                 for r in self.ROLES if pos[r] < len(self.streams[r])}
        assert not stuck, f"DEADLOCK at {stuck} (pos={pos})"


def _carve(arena, off, nbytes, dtype, p0=0, p1=128):
    return arena[p0:p1, off:off + nbytes].bitcast(dtype)


def build_nc(debug=False, stop_after=None):
    nc = bass.Bass("TRN2", target_bir_lowering=False)

    xT = nc.dram_tensor("xT", [2048, NTOK], F32, kind="ExternalInput").ap()
    wqT = nc.dram_tensor("wqT", [2048, 2048], F32, kind="ExternalInput").ap()
    wkT = nc.dram_tensor("wkT", [2048, 512], F32, kind="ExternalInput").ap()
    wvT = nc.dram_tensor("wvT", [2048, 512], F32, kind="ExternalInput").ap()
    woT = nc.dram_tensor("woT", [2048, 2048], F32, kind="ExternalInput").ap()
    bq = nc.dram_tensor("bq", [1, 2048], F32, kind="ExternalInput").ap()
    bk = nc.dram_tensor("bk", [1, 512], F32, kind="ExternalInput").ap()
    bv = nc.dram_tensor("bv", [1, 512], F32, kind="ExternalInput").ap()
    wg = nc.dram_tensor("wg", [32, 128, 16, 256], I8, kind="ExternalInput").ap()
    wu = nc.dram_tensor("wu", [64, 128, 16, 128], I8, kind="ExternalInput").ap()
    wd = nc.dram_tensor("wd", [8192, 2048], I8, kind="ExternalInput").ap()
    sg = nc.dram_tensor("sg", [128, 64], F32, kind="ExternalInput").ap()
    su = nc.dram_tensor("su", [128, 64], F32, kind="ExternalInput").ap()
    sd = nc.dram_tensor("sd", [128, 16], F32, kind="ExternalInput").ap()
    out = nc.dram_tensor("out", [2048, NTOK], F32, kind="ExternalOutput").ap()

    sq_sp = nc.dram_tensor("sq_sp", [128, NT * 8192], BF16,
                           kind="ExternalOutput" if debug else "Internal").ap()
    x1_sp = nc.dram_tensor("x1_sp", [128, NT * 8192], F32,
                           kind="ExternalOutput" if debug else "Internal").ap()
    cc_in_h = nc.dram_tensor("cc_in", [128, 516], F32, kind="Internal")
    cc_out_h = nc.dram_tensor("cc_out", [128, 516], F32, kind="Internal")
    cc_dbg_h = (nc.dram_tensor("cc_dbg", [128, 516], F32, kind="ExternalOutput")
                if debug else None)

    bld = Bld()
    bld.nc = nc
    b = bld
    with ExitStack() as ctx:
        e = ctx.enter_context
        arena_h = e(nc.sbuf_tensor("arena", [128, ARENA_BYTES], U8))
        ar = arena_h.ap()
        ones_col = e(nc.sbuf_tensor("ones_col", [128, 1], BF16))
        ones128 = e(nc.sbuf_tensor("ones128", [1, 128], BF16))
        sg_sb = e(nc.sbuf_tensor("sg_sb", [128, 64], F32))
        su_sb = e(nc.sbuf_tensor("su_sb", [128, 64], F32))
        sd_sb = e(nc.sbuf_tensor("sd_sb", [128, 16], F32))
        pb = [e(nc.psum_tensor(f"pb{i}", [128, 512], F32)) for i in range(8)]
        for s in bld.sems.values():
            s.h = e(nc.semaphore(s.name))
        block = e(nc.Block())

        wq_sb = _carve(ar, O_WQ, 65536, BF16)
        wk_sb = _carve(ar, O_WK, 16384, BF16)
        wv_sb = _carve(ar, O_WV, 16384, BF16)
        wo_sb = _carve(ar, O_WO, 65536, BF16)
        sq2 = [_carve(ar, o, 16384, BF16) for o in O_SQ2]
        f_sb = _carve(ar, O_F, 65536, BF16)
        g_sb = [_carve(ar, o, 8192, BF16) for o in O_G]
        u_sb = [_carve(ar, o, 4096, BF16) for o in O_U]
        wd_sb = [_carve(ar, o, 8192, BF16) for o in O_WD]
        x1sq = [_carve(ar, o, 1024, BF16) for o in O_X1SQ]
        dsq4 = [_carve(ar, o, 1024, BF16) for o in O_DSQ4]
        rbx3 = _carve(ar, O_RBX3, 1024, BF16)
        nbx3 = _carve(ar, O_NBX3, 1024, BF16)
        xb = [_carve(ar, o, 16384, BF16) for o in O_XB]
        xn = _carve(ar, O_XN, 16384, BF16)
        sq1 = _carve(ar, O_SQ1, 16384, BF16)
        x2in = _carve(ar, O_X2IN, 32768, F32)
        tmp2 = [_carve(ar, o, 1024, BF16) for o in O_TMP2]
        x1f = _carve(ar, O_X1F, 32768, F32)
        x1b = _carve(ar, O_X1B, 16384, BF16)
        x1n = _carve(ar, O_X1N, 16384, BF16)
        sk_sb = _carve(ar, O_SK, 4096, BF16)
        vx_sb = _carve(ar, O_VX, 4128, BF16)
        kve = _carve(ar, O_KVE, 2064, F32)
        eqmn = [_carve(ar, o, 2048, F32) for o in O_EQMN]
        eqex = [_carve(ar, o, 1024, BF16) for o in O_EQEX]
        eqmx = [_carve(ar, o, 1024, BF16) for o in O_EQMX]
        apt1 = [_carve(ar, o, 1024, BF16) for o in O_APT1]
        att = _carve(ar, O_ATT, 16384, BF16)
        attb = _carve(ar, O_ATTB, 16384, BF16)
        aob = _carve(ar, O_AOB, 16384, BF16)
        ds = _carve(ar, O_DS, 16384, BF16)
        us_s = [_carve(ar, o, 1024, BF16) for o in O_US]
        fs_s = [_carve(ar, o, 1024, BF16) for o in O_FS]
        x3t = [_carve(ar, o, 1024, BF16) for o in O_X3T]
        xsq = [_carve(ar, o, 1024, BF16) for o in O_XSQ]
        rbx1 = _carve(ar, O_RBX1, 1024, BF16)
        nbx1 = _carve(ar, O_NBX1, 1024, BF16)
        aosq = [_carve(ar, o, 1024, BF16) for o in O_AOSQ]
        x1c = [_carve(ar, o, 2048, F32) for o in O_X1C]
        rb2 = _carve(ar, O_RB2, 1024, BF16)
        nb2 = _carve(ar, O_NB2, 1024, BF16)
        kv_sb = _carve(ar, O_KVSB, 1032, BF16)
        s1_sb = _carve(ar, O_S1, 2048, F32)
        ob = [_carve(ar, o, 2048, F32) for o in O_OB]
        rbd = _carve(ar, O_RBD, 1024, BF16)
        nbd = _carve(ar, O_NBD, 1024, BF16)
        aob0b = _carve(ar, O_AOB0B, 1024, BF16)

        def row(off, nbytes, dtype, lane=0):
            return _carve(ar, off, nbytes, dtype, p0=lane, p1=lane + 1)

        bq_bf = row(RW1_BQ, 4096, BF16)
        r1rbf = row(RW1_RBF, 1024, BF16)
        r1nbf = row(RW1_NBF, 1024, BF16)
        ones_row = row(RW1_ONE, 1024, BF16)
        r1m = row(RW1_M, 2048, F32, 32)
        r1q = row(RW1_Q, 2048, F32, 32)
        r1l = row(RW1_L, 2048, F32, 32)
        r1r = row(RW1_R, 2048, F32, 32)
        bk_bf = row(RW1_BK, 1024, BF16)
        bv_bf = row(RW1_BV, 1024, BF16)
        stg = row(RW1_STG, 4096, F32, 64)
        r2rbf = row(RW2_RBF, 1024, BF16)
        r2nbf = row(RW2_NBF, 1024, BF16)
        recbf = [row(RW2_RECBF, 1024, BF16), row(RW2_RECBF2, 1024, BF16)]
        r2m = row(RW2_M, 2048, F32, 32)
        r2q = row(RW2_Q, 2048, F32, 32)
        r2l = row(RW2_L, 2048, F32, 32)
        r2r = row(RW2_R, 2048, F32, 32)
        r2norm = row(RW2_NORM, 2048, F32, 64)
        r2rec = row(RW2_REC, 2048, F32, 64)
        r3drbf = row(RW3_DRBF, 1024, BF16)
        r3dnbf = row(RW3_DNBF, 1024, BF16)
        r3xrbf = row(RW3_XRBF, 1024, BF16)
        r3xnbf = row(RW3_XNBF, 1024, BF16)
        r3m = row(RW3_M, 2048, F32, 32)
        r3q = row(RW3_Q, 2048, F32, 32)
        r3l = row(O_RL3, 2048, F32, 32)
        r3r = row(RW3_RR, 2048, F32, 32)

        def ch(ap, c, w=512):
            return ap[:, c * w:(c + 1) * w]

        xTr = xT.rearrange("(c p) t -> p c t", p=128)

        def rows_chain(tag, ps_sum, ps_sq, ev_sum, ev_sq, rm, rq, rl, rr,
                       rbf, nbf, waits_extra=()):
            b.wait("vector", ev_sum)
            b.waits("vector", list(waits_extra))
            b.ts(rm, ps_sum[0:1, :], INV_D, ALU.mult, ev=f"{tag}_rm")
            b.wait("vector", ev_sq)
            b.ts(rq, ps_sq[0:1, :], INV_D, ALU.mult, ev=f"{tag}_rq")
            b.wait("vector", f"{tag}_rm")
            b.tt(rl, rm, rm, ALU.mult, ev=f"{tag}_rmm")
            b.wait("vector", f"{tag}_rmm")
            b.stt(rq, rq, EPS_LN, rl, ALU.add, ALU.subtract, ev=f"{tag}_rv")
            b.wait("scalar", f"{tag}_rv")
            b.act(rl, rq, AF.Ln, ev=f"{tag}_rln")
            b.wait("scalar", f"{tag}_rln")
            b.act(rr, rl, AF.Exp, scale=-0.5, ev=f"{tag}_rr")
            b.wait("vector", f"{tag}_rr")
            b.stt(rm, rm, -1.0, rr, ALU.mult, ALU.mult, ev=f"{tag}_rn")
            b.wait("scalar", f"{tag}_rr")
            b.act(rbf, rr, AF.Copy, ev=f"{tag}_rbf")
            b.wait("scalar", f"{tag}_rn")
            b.act(nbf, rm, AF.Copy, ev=f"{tag}_rows")

        rot1 = []

        def acquire1(free_evs):
            idx = len(rot1)
            bank = idx % 4
            if idx >= 4:
                for ev2 in rot1[idx - 4]:
                    b.wait("tensor", ev2)
            rot1.append(free_evs)
            return pb[bank]

        def _emit_q_group(tq, m):
            pq = acquire1([f"eqmn_{tq}_{m}", f"eqmx_{tq}_{m}"])
            if tq == 0 and m == 0:
                b.wait("tensor", "wq_ld")
            b.wait("tensor", f"xn_{tq}_{NCH - 1}")
            for kc in range(NCH):
                b.mm(pq[:], wq_sb[:, kc * 2048 + m * 128: kc * 2048 + (m + 1) * 128],
                     ch(xn, kc), start=(kc == 0), stop=False)
            b.mm(pq[:], bq_bf[0:1, m * 128:(m + 1) * 128], ones_row,
                 start=False, stop=True, ev=f"q_{tq}_{m}")
            b.wait("vector", f"q_{tq}_{m}")
            if m >= 2:
                b.wait("vector", f"eqex_{tq}_{m - 2}")
            else:
                b.wait("vector", f"ekex_{tq}_{m + 2}")
            b.ts(eqmn[m % 2], pq[:], 0.0, ALU.min, ev=f"eqmn_{tq}_{m}")
            b.wait("scalar", f"eqmn_{tq}_{m}")
            if m >= 2:
                b.wait("scalar", f"sqa_{tq}_{m - 2}")
            else:
                b.wait("scalar", f"sk_{tq}_{m + 2}")
            b.act(eqex[m % 2], eqmn[m % 2][:], AF.Exp, ev=f"eqex_{tq}_{m}")
            b.wait("scalar", f"q_{tq}_{m}")
            b.act(eqmx[m % 2], pq[:], AF.Relu, ev=f"eqmx_{tq}_{m}")
            b.waits("vector", [f"eqex_{tq}_{m}", f"eqmx_{tq}_{m}"])
            if tq > 0:
                b.wait("vector", f"st_sq_{tq - 1}")
            b.tt(ch(sq1, m), eqex[m % 2][:], eqmx[m % 2][:], ALU.add,
                 ev=f"sqa_{tq}_{m}")

        def _emit_kv_phase(tq):
            if tq == 0:
                b.waits("tensor", ["wk_ld", "wv_ld"])
            b.wait("tensor", f"xn_{tq}_{NCH - 1}")
            for mu in range(4):
                pk = acquire1([f"ekmn_{tq}_{mu}", f"ekmx_{tq}_{mu}"])
                for kc in range(NCH):
                    b.mm(pk[:], xn[:, kc * 512 + mu * 128: kc * 512 + (mu + 1) * 128],
                         ch(wk_sb, kc), start=(kc == 0), stop=False)
                b.mm(pk[:], ones128[:], bk_bf[0:1, 0:512], start=False, stop=True,
                     ev=f"k_{tq}_{mu}")
                b.wait("vector", f"k_{tq}_{mu}")
                if mu < 2:
                    if tq > 0:
                        b.wait("vector", f"eqex_{tq - 1}_{14 + mu}")
                else:
                    b.wait("vector", f"ekex_{tq}_{mu - 2}")
                b.ts(eqmn[mu % 2], pk[:], 0.0, ALU.min, ev=f"ekmn_{tq}_{mu}")
                b.wait("scalar", f"ekmn_{tq}_{mu}")
                if mu < 2:
                    if tq > 0:
                        b.wait("scalar", f"sqa_{tq - 1}_{14 + mu}")
                else:
                    b.wait("scalar", f"sk_{tq}_{mu - 2}")
                b.act(eqex[mu % 2], eqmn[mu % 2][:], AF.Exp, ev=f"ekex_{tq}_{mu}")
                b.wait("scalar", f"k_{tq}_{mu}")
                b.act(eqmx[mu % 2], pk[:], AF.Relu, ev=f"ekmx_{tq}_{mu}")
                b.waits("vector", [f"ekex_{tq}_{mu}", f"ekmx_{tq}_{mu}"])
                if tq > 0:
                    b.wait("vector", f"kv_{tq - 1}_{mu}")
                b.tt(ch(sk_sb, mu), eqex[mu % 2][:], eqmx[mu % 2][:], ALU.add,
                     ev=f"sk_{tq}_{mu}")
                pv = acquire1([f"vx_{tq}_{mu}"])
                for kc in range(NCH):
                    b.mm(pv[:], xn[:, kc * 512 + mu * 128: kc * 512 + (mu + 1) * 128],
                         ch(wv_sb, kc), start=(kc == 0), stop=False)
                b.mm(pv[:], ones128[:], bv_bf[0:1, 0:512], start=False, stop=True,
                     ev=f"v_{tq}_{mu}")
                b.wait("vector", f"v_{tq}_{mu}")
                if tq > 0:
                    b.wait("vector", f"kv_{tq - 1}_{mu}")
                b.vcopy(vx_sb.rearrange("p (h c) -> p h c", c=129)
                        [:, 4 * mu:4 * (mu + 1), 0:128],
                        pv[:].rearrange("p (h c) -> p h c", c=128),
                        ev=f"vx_{tq}_{mu}")
            for mu in range(4):
                b.waits("tensor", [f"sk_{tq}_{mu}", f"vx_{tq}_{mu}"])
                for kk in range(KVH):
                    g = mu * 4 + kk
                    last = (tq == NT - 1 and mu == 3 and kk == KVH - 1)
                    b.mm(pb[4 + kk][:, 0:129],
                         sk_sb[:, mu * 512 + kk * 128: mu * 512 + (kk + 1) * 128],
                         vx_sb[:, g * 129:(g + 1) * 129],
                         start=(tq == 0 and mu == 0), stop=(tq == NT - 1 and mu == 3),
                         ev=("kv_done" if last else
                             (f"kv_{tq}_{mu}" if kk == KVH - 1 else None)))

        def _emit_p2_loads(t):
            b.wait("sync", f"st_sq_{t}")
            if t >= 2:
                b.wait("sync", f"nm_{t - 2}_{H - 1}")
            else:
                b.wait("sync", "vx_7_3")
            b.dma("sync", sq2[t % 2][:], sq_sp[:, t * 8192:(t + 1) * 8192],
                  f"sq2_{t}", f"sSq2{t % 2}")

        def _head_front(t, h):
            at_buf = att if t % 2 == 0 else attb
            kk = h // 4
            pq = pb[h % 2]
            if t == 0 and h == 0:
                b.wait("tensor", "kv_ld")
                b.wait("tensor", "sqa_7_15")
            b.wait("tensor", f"sq2_{t}")
            if h >= 2:
                b.wait("tensor", f"aevac_{t}_{h - 2}")
            elif t > 0:
                b.wait("tensor", f"aevac_{t - 1}_{h + 14}")
            b.mm(pq[:], kv_sb[:, kk * 129: kk * 129 + 128], ch(sq2[t % 2], h),
                 start=True, stop=True, ev=f"qkv_{t}_{h}")
            b.wait("scalar", f"qkv_{t}_{h}")
            if h == 0 and t >= 2:
                b.wait("scalar", f"op_{t - 2}_{NCH - 1}")
            if h == 0 and t == 1:
                b.wait("scalar", f"st_sq_{NT - 1}")
            if h == 8 and t == 0:
                b.waits("scalar", ["cc_st", "q_7_15"])
            b.act(ch(at_buf, h), pq[:], AF.Copy, ev=f"aevac_{t}_{h}")
            pn = pb[2]
            if h >= 1:
                b.wait("tensor", f"ne_{t}_{h - 1}")
            elif t > 0:
                b.wait("tensor", f"ne_{t - 1}_{H - 1}")
            b.mm(pn[0:1, :], kv_sb[:, kk * 129 + 128: kk * 129 + 129],
                 ch(sq2[t % 2], h), start=True, stop=True, ev=f"nm_{t}_{h}")
            b.wait("vector", f"nm_{t}_{h}")
            if h >= 1:
                b.wait("vector", f"lnn_{t}_{h - 1}")
            elif t > 0:
                b.wait("vector", f"lnn_{t - 1}_{H - 1}")
            b.ts(r2norm, pn[0:1, :], EPS_ATTN, ALU.add, ev=f"ne_{t}_{h}")
            b.wait("scalar", f"ne_{t}_{h}")
            if h >= 1:
                b.wait("scalar", f"rec_{t}_{h - 1}")
            elif t > 0:
                b.wait("scalar", f"rec_{t - 1}_{H - 1}")
            b.act(r2rec, r2norm, AF.Ln, ev=f"lnn_{t}_{h}")
            b.wait("scalar", f"lnn_{t}_{h}")
            if h >= 2:
                b.wait("scalar", f"bcr_{t}_{h - 2}")
            elif t > 0:
                b.wait("scalar", f"bcr_{t - 1}_{h + 14}")
            b.act(recbf[h % 2], r2rec, AF.Exp, scale=-1.0, ev=f"rec_{t}_{h}")

        def _head_back(t, h):
            at_buf = att if t % 2 == 0 else attb
            pc = pb[3]
            b.wait("tensor", f"rec_{t}_{h}")
            if h >= 1:
                b.wait("tensor", f"at_{t}_{h - 1}")
            elif t > 0:
                b.wait("tensor", f"at_{t - 1}_{H - 1}")
            b.mm(pc[:], ones128[:], recbf[h % 2], start=True, stop=True,
                 ev=f"bcr_{t}_{h}")
            b.waits("vector", [f"aevac_{t}_{h}", f"bcr_{t}_{h}"])
            if h == 0 and t >= 2:
                b.wait("vector", f"op_{t - 2}_{NCH - 1}")
            b.tt(ch(at_buf, h), ch(at_buf, h), pc[:], ALU.mult, ev=f"at_{t}_{h}")

        def _op_step(t, m):
            at_buf = att if t % 2 == 0 else attb
            po = pb[4 + m % 2]
            if m == 0:
                b.wait("tensor", f"at_{t}_{H - 1}")
                if t == 0:
                    b.waits("tensor", ["wo_ld2", "wo_ld"])
            if m >= 2:
                b.waits("tensor", [f"aoe_{t}_{m - 2}", f"aosq_{t}_{m - 2}"])
            elif t > 0:
                b.waits("tensor", [f"aoe_{t - 1}_{m + 14}",
                                   f"aosq_{t - 1}_{m + 14}"])
            for kc in range(NCH):
                b.mm(po[:], wo_sb[:, kc * 2048 + m * 128: kc * 2048 + (m + 1) * 128],
                     ch(at_buf, kc), start=(kc == 0), stop=(kc == NCH - 1),
                     ev=(f"op_{t}_{m}" if kc == NCH - 1 else None))
            b.wait("scalar", f"op_{t}_{m}")
            if m == 0:
                if t >= 2:
                    b.wait("scalar", f"t2a_{t - 2}_0")
            elif t > 0:
                b.wait("scalar", f"t2a_{t - 1}_{m}")
            b.act(aobch(t, m), po[:], AF.Copy, ev=f"aoe_{t}_{m}")
            if m >= 2:
                b.wait("scalar", f"aoqm_{t}_{m - 2}")
            elif t > 0:
                b.wait("scalar", f"aoqm_{t - 1}_{m + 14}")
            b.act(aosq[m % 2], po[:], AF.Square, ev=f"aosq_{t}_{m}")

        def _op_stats(t, m):
            b.wait("tensor", f"aoe_{t}_{m}")
            if m == 0 and t > 0:
                b.wait("tensor", f"nb2e_{t - 1}")
            b.mm(pb[6][0:1, :], ones_col[:], aobch(t, m), start=(m == 0),
                 stop=(m == NCH - 1),
                 ev=(f"aos_{t}" if m == NCH - 1 else None))
            b.wait("tensor", f"aosq_{t}_{m}")
            b.mm(pb[7][0:1, :], ones_col[:], aosq[m % 2][:], start=(m == 0),
                 stop=(m == NCH - 1), ev=f"aoqm_{t}_{m}")

        def _emit_pass2_tail(t):
            rows_chain(f"p2_{t}", pb[6], pb[7], f"aos_{t}", f"aoqm_{t}_{NCH - 1}",
                       r2m, r2q, r2l, r2r, r2rbf, r2nbf)
            b.waits("tensor", [f"p2_{t}_rbf", f"p2_{t}_rm", f"p2_{t}_rq"])
            b.mm(pb[6][:], ones128[:], r2rbf, start=True, stop=True, ev=f"bcR2_{t}")
            b.wait("tensor", f"p2_{t}_rows")
            b.mm(pb[7][:], ones128[:], r2nbf, start=True, stop=True, ev=f"bcN2_{t}")
            b.wait("scalar", f"bcR2_{t}")
            b.act(rb2[:], pb[6][:], AF.Copy, ev=f"rb2e_{t}")
            b.wait("scalar", f"bcN2_{t}")
            b.act(nb2[:], pb[7][:], AF.Copy, ev=f"nb2e_{t}")

        def aobch(t, kc):
            return aob0b[:] if (kc == 0 and t % 2 == 1) else ch(aob, kc)

        def _apply_step(t, kc):
            if kc == 0:
                b.waits("vector", [f"rb2e_{t}", f"nb2e_{t}", f"x_{t}"])
            b.tt(tmp2[kc % 2], aobch(t, kc), rb2[:], ALU.mult, ev=f"t2a_{t}_{kc}")
            if kc >= 2:
                b.wait("vector", f"stx1_{t}_{kc - 2}")
            elif t > 0:
                b.wait("vector", f"stx1_{t - 1}_{kc + 14}")
            b.tt(x1c[kc % 2], ch(x2in, kc), nb2[:], ALU.add, ev=f"t2b_{t}_{kc}")
            b.wait("vector", f"t2b_{t}_{kc}")
            b.tt(x1c[kc % 2], tmp2[kc % 2][:], x1c[kc % 2][:], ALU.add,
                 ev=f"x1c_{t}_{kc}")
            b.wait("sync", f"x1c_{t}_{kc}")
            b.dma("sync", x1_sp[:, t * 8192 + kc * 512: t * 8192 + (kc + 1) * 512],
                  x1c[kc % 2][:], f"stx1_{t}_{kc}", f"sSx1{kc % 4}")

        def _x1_stats_chunk(t, c, idx=None):
            if idx is None:
                idx = c
            first = (idx == 0)
            last = (idx == NCH - 1)
            if first:
                b.wait("tensor", f"x1b_{t}" if t > 0 else "x1b_0a")
                if t == 0:
                    b.waits("tensor", [f"aosq_{NT - 1}_{NCH - 1}",
                                       f"nb2e_{NT - 1}"])
                elif t == 1:
                    b.wait("tensor", "nbx3e_0")
                else:
                    b.wait("tensor", f"dse_{t - 2}_12")
                    b.wait("tensor", f"dse_{t - 2}_13")
            if t == 0:
                b.wait("vector", "x1b_0a" if c >= 8 else "x1b_0b")
                if idx == 8:
                    b.wait("tensor", "x1b_0b")
            else:
                b.wait("vector", f"x1b_{t}")
            if idx >= 4:
                b.wait("vector", f"x1sqw_{t}_{idx - 4}")
            elif t > 0:
                b.wait("vector", f"x1sqw_{t - 1}_{idx + 12}")
            b.tt(x1sq[idx % 4], ch(x1b, c), ch(x1b, c), ALU.mult,
                 ev=f"x1sq_{t}_{c}")
            b.mm(pb[4][0:1, :], ones_col[:], ch(x1b, c),
                 start=first, stop=last,
                 ev=(f"x1s_{t}" if last else None))
            b.wait("tensor", f"x1sq_{t}_{c}")
            b.mm(pb[5][0:1, :], ones_col[:], x1sq[idx % 4][:],
                 start=first, stop=last, ev=f"x1sqw_{t}_{idx}")
            if last:
                if b.dry:
                    b.events[f"x1q_{t}_done"] = b.events[f"x1sqw_{t}_{idx}"]

        def _x1_ln(t):
            for c in range(NCH):
                _x1_stats_chunk(t, c)
            _x1_rows(t)

        def _x1_rows(t):
            extra = ([f"p3d_{t - 2}_rows"] if t >= 2 else
                     (["p3x_0_rows"] if t == 1 else
                      [f"stx1_{NT - 1}_{NCH - 2}", f"stx1_{NT - 1}_{NCH - 1}"]))
            rows_chain(f"p3x_{t}", pb[4], pb[5], f"x1s_{t}", f"x1q_{t}_done",
                       r3m, r3q, r3l, r3r, r3xrbf, r3xnbf, waits_extra=extra)

        def _x1_bcast(t):
            b.waits("tensor", [f"p3x_{t}_rbf", f"p3x_{t}_rm", f"p3x_{t}_rq"])
            b.mm(pb[4][:], ones128[:], r3xrbf, start=True, stop=True, ev=f"bcRx_{t}")
            b.wait("tensor", f"p3x_{t}_rows")
            b.mm(pb[5][:], ones128[:], r3xnbf, start=True, stop=True, ev=f"bcNx_{t}")
            b.wait("scalar", f"bcRx_{t}")
            b.act(rbx3[:], pb[4][:], AF.Copy, ev=f"rbx3e_{t}")
            b.wait("scalar", f"bcNx_{t}")
            b.act(nbx3[:], pb[5][:], AF.Copy, ev=f"nbx3e_{t}")

        def _x1_ln_apply(t, apply_wait=None):
            b.waits("vector", [f"rbx3e_{t}", f"nbx3e_{t}"])
            if apply_wait is not None:
                b.wait("vector", apply_wait)
            for c in range(NCH):
                b.tt(x3t[c % 2], ch(x1b, c), rbx3[:], ALU.mult, ev=f"x3a_{t}_{c}")
                b.wait("vector", f"x3a_{t}_{c}")
                b.tt(ch(x1n, c), x3t[c % 2][:], nbx3[:], ALU.add, ev=f"x1n_{t}_{c}")

        def _d_stats(t):
            for c in range(NCH):
                b.wait("tensor", f"dse_{t}_{c}")
                if c == 0:
                    b.wait("tensor", f"dse_{t}_14")
                    b.wait("tensor", f"dse_{t}_15")
                    if t > 0:
                        b.wait("tensor", f"nbde_{t - 1}")
                b.mm(pb[6][0:1, :], ones_col[:], ch(ds, c), start=(c == 0),
                     stop=(c == NCH - 1),
                     ev=(f"dst_{t}" if c == NCH - 1 else f"dss_{t}_{c}"))
                b.wait("vector", f"dse_{t}_{c}")
                if c >= 4:
                    b.wait("vector", f"dsq_mm_{t}_{c - 4}")
                elif t > 0:
                    b.wait("vector", f"dsq_mm_{t - 1}_{c + 12}")
                b.tt(dsq4[c % 4], ch(ds, c), ch(ds, c), ALU.mult,
                     ev=f"dsq_{t}_{c}")
                b.wait("tensor", f"dsq_{t}_{c}")
                b.mm(pb[7][0:1, :], ones_col[:], dsq4[c % 4][:], start=(c == 0),
                     stop=(c == NCH - 1), ev=f"dsq_mm_{t}_{c}")
            extra = [f"p3x_{t + 1}_rows"] if t + 1 < NT else []
            rows_chain(f"p3d_{t}", pb[6], pb[7], f"dst_{t}", f"dsq_mm_{t}_{NCH - 1}",
                       r3m, r3q, r3l, r3r, r3drbf, r3dnbf, waits_extra=extra)

        def _d_bcast(t):
            b.waits("tensor", [f"p3d_{t}_rbf", f"p3d_{t}_rm", f"p3d_{t}_rq"])
            b.mm(pb[6][:], ones128[:], r3drbf, start=True, stop=True, ev=f"bcRd_{t}")
            b.wait("tensor", f"p3d_{t}_rows")
            b.mm(pb[7][:], ones128[:], r3dnbf, start=True, stop=True, ev=f"bcNd_{t}")
            b.wait("scalar", f"bcRd_{t}")
            b.act(rbd[:], pb[6][:], AF.Copy, ev=f"rbde_{t}")
            b.wait("scalar", f"bcNd_{t}")
            b.act(nbd[:], pb[7][:], AF.Copy, ev=f"nbde_{t}")

        def _d_apply_chunk(t, c):
            if c == 0:
                b.waits("vector", [f"rbde_{t}", f"nbde_{t}", f"x1f_{t}"])
            b.tt(s1_sb[:], ch(ds, c), rbd[:], ALU.mult, ev=f"d3a_{t}_{c}")
            if c >= 2:
                b.wait("vector", f"st_out_{t}_{c - 2}")
            elif t > 0:
                b.wait("vector", f"st_out_{t - 1}_{c + 14}")
            b.wait("vector", f"d3a_{t}_{c}")
            b.tt(ob[c % 2], s1_sb[:], ch(x1f, c), ALU.add, ev=f"d3b_{t}_{c}")
            b.wait("vector", f"d3b_{t}_{c}")
            b.tt(ob[c % 2], ob[c % 2][:], nbd[:], ALU.add, ev=f"x2_{t}_{c}")
            b.wait("sync", f"x2_{t}_{c}")
            b.dma("sync", out[c * 128:(c + 1) * 128, t * T:(t + 1) * T],
                  ob[c % 2][:], f"st_out_{t}_{c}", f"sSo{c % 4}")

        def _emit_pass3_tile(t):
            rot3 = []

            def acquire3(free_evs):
                idx = len(rot3)
                bank = idx % 4
                if idx >= 4:
                    for ev2 in rot3[idx - 4]:
                        b.wait("tensor", ev2)
                elif t > 0:
                    b.wait("tensor", f"dse_{t - 1}_{8 + bank}")
                else:
                    b.wait("tensor", [f"at_{NT - 1}_{14}", f"at_{NT - 1}_{15}",
                                      f"ne_{NT - 1}_{H - 1}",
                                      f"at_{NT - 1}_{H - 1}"][bank])
                rot3.append(free_evs)
                return pb[bank]

            if t + 1 < NT:
                b.waits("gpsimd", [f"x1n_{t}_{NCH - 1}", f"x1sq_{t}_{NCH - 1}"])
                b.dma("gpsimd", x1b.rearrange("p (c t) -> p c t", t=T),
                      x1_sp.rearrange("p (tt ct) -> p tt ct", ct=512)
                      [:, (t + 1) * NCH:(t + 2) * NCH, :], f"x1b_{t + 1}",
                      "sGx1b")
            for j in range(32):
                if j + 2 < 32:
                    b.wait("gpsimd", f"gg_{t}_{j}_1")
                    b.dma("gpsimd", g_sb[j % 2].rearrange("p (kc i) -> p kc i", i=256),
                          wg[j + 2], f"g_ld_{t}_{j + 2}", f"sGg{j % 2}")
                for iu in (2 * j + 2, 2 * j + 3):
                    if iu < 64:
                        b.wait("gpsimd", f"uu_{t}_{(iu - 2) // 2}_{(iu - 2) % 2}")
                        b.dma("gpsimd",
                              u_sb[iu % 2].rearrange("p (kc i) -> p kc i", i=128),
                              wu[iu], f"u_ld_{t}_{iu}", f"sGu{iu % 2}" if iu % 2 else "sGu")

                for ml in range(2):
                    ic = j * 2 + ml
                    pg = acquire3([f"sil_{t}_{ic}"])
                    b.waits("tensor", [f"g_ld_{t}_{j}", f"x1n_{t}_{NCH - 1}"])
                    for kc in range(NCH):
                        b.mm(pg[:], g_sb[j % 2][:, kc * 256 + ml * 128:
                                               kc * 256 + (ml + 1) * 128],
                             ch(x1n, kc), start=(kc == 0), stop=(kc == NCH - 1),
                             ev=(f"gg_{t}_{j}_{ml}" if kc == NCH - 1 else None))
                    pu = acquire3([f"usc_{t}_{ic}"])
                    b.wait("tensor", f"u_ld_{t}_{ic}")
                    for kc in range(NCH):
                        b.mm(pu[:], u_sb[ic % 2][:, kc * 128:(kc + 1) * 128],
                             ch(x1n, kc), start=(kc == 0), stop=(kc == NCH - 1),
                             ev=(f"uu_{t}_{j}_{ml}" if kc == NCH - 1 else None))
                    b.wait("scalar", f"gg_{t}_{j}_{ml}")
                    if ic >= 2:
                        b.wait("scalar", f"f_{t}_{ic - 2}")
                    elif t > 0:
                        b.wait("scalar", f"f_{t - 1}_{ic + 62}")
                    else:
                        b.wait("scalar", f"x1c_{NT - 1}_{NCH - 1}")
                    b.act(fs_s[ic % 2][:], pg[:], AF.Silu, scale=sg_sb[:, ic:ic + 1],
                          ev=f"sil_{t}_{ic}")
                    b.wait("scalar", f"uu_{t}_{j}_{ml}")
                    b.act(us_s[ic % 2][:], pu[:], AF.Copy, scale=su_sb[:, ic:ic + 1],
                          ev=f"usc_{t}_{ic}")
                    b.waits("vector", [f"sil_{t}_{ic}", f"usc_{t}_{ic}"])
                    if ic == 0:
                        if t > 0:
                            b.wait("vector", f"dn_{t - 1}_1_7")
                        else:
                            b.wait("vector", f"op_{NT - 1}_{NCH - 1}")
                    b.tt(ch(f_sb, ic), fs_s[ic % 2][:], us_s[ic % 2][:], ALU.mult,
                         ev=f"f_{t}_{ic}")

                if j == 0 and t > 0:
                    _d_stats(t - 1)
                if j == 2 and t > 0:
                    _d_bcast(t - 1)
                if 4 <= j < 4 + NCH and t > 0:
                    _d_apply_chunk(t - 1, j - 4)
                if j == 3 and t + 1 < NT:
                    _x1_ln(t + 1)
                if j == 5 and t + 1 < NT:
                    _x1_bcast(t + 1)
                if j == 21 and t > 0:
                    b.wait("sync", f"x2_{t - 1}_{NCH - 1}")
                    b.dma("sync", x1f[:], x1_sp[:, t * 8192:(t + 1) * 8192],
                          f"x1f_{t}", "sSx1f")

            if t + 1 < NT:
                _x1_ln_apply(t + 1, apply_wait=f"uu_{t}_31_1")
            wdr = wd.rearrange("(ic p) o -> p ic o", p=128)
            for half in range(2):
                for icb in range(16):
                    par = icb % 2
                    if half == 0 and icb < 2:
                        b.wait("gpsimd", f"gg_{t}_30_1" if icb == 0
                               else f"gg_{t}_31_1")
                    else:
                        b.wait("gpsimd", f"dnb_{t}_{half}_{icb - 2}" if icb >= 2
                               else (f"dnb_{t}_0_14" if icb == 0
                                     else f"dn_{t}_0_7"))
                    b.dma("gpsimd",
                          wd_sb[par].rearrange("p (ic o) -> p ic o", o=1024),
                          wdr[:, icb * 4:(icb + 1) * 4,
                              half * 1024:(half + 1) * 1024],
                          f"wd_{t}_{half}_{icb}", f"sGwd{par}")
                    b.wait("tensor", f"wd_{t}_{half}_{icb}")
                    if half == 0 and icb == 0:
                        b.wait("tensor", f"f_{t}_63")
                        b.waits("tensor", [f"sil_{t}_62", f"usc_{t}_62",
                                           f"sil_{t}_63", f"usc_{t}_63"])
                        if t + 1 < NT:
                            b.waits("tensor", [f"rbx3e_{t + 1}", f"nbx3e_{t + 1}"])
                        if t > 0:
                            b.waits("tensor", [f"rbde_{t - 1}", f"nbde_{t - 1}"])
                    for icl in range(4):
                        ic = icb * 4 + icl
                        for o in range(8):
                            if half == 1 and ic == 0:
                                b.wait("tensor", f"dse_{t}_{o}")
                            last = (ic == 63)
                            b.mm(pb[o][:],
                                 wd_sb[par][:, icl * 1024 + o * 128:
                                            icl * 1024 + (o + 1) * 128],
                                 ch(f_sb, ic), start=(ic == 0), stop=last,
                                 ev=(f"dn_{t}_{half}_{o}" if last else
                                     (f"dnb_{t}_{half}_{icb}"
                                      if (icl == 3 and o == 7) else None)))
                for o in range(8):
                    c = half * 8 + o
                    role2 = "scalar" if o % 2 == 0 else "vector"
                    b.wait(role2, f"dn_{t}_{half}_{o}")
                    if half == 0 and o in (0, 1) and t > 0:
                        b.wait(role2, f"x2_{t - 1}_{NCH - 1}")
                        b.wait(role2, f"dsq_mm_{t - 1}_{NCH - 1}")
                    if o % 2 == 0:
                        b.act(ch(ds, c), pb[o][:], AF.Copy,
                              scale=sd_sb[:, c:c + 1], ev=f"dse_{t}_{c}")
                    else:
                        b.ts(ch(ds, c), pb[o][:], sd_sb[:, c:c + 1], ALU.mult,
                             ev=f"dse_{t}_{c}")

            if t + 1 < NT:
                b.wait("gpsimd", f"dnb_{t}_1_14")
                b.dma("gpsimd", g_sb[0].rearrange("p (kc i) -> p kc i", i=256),
                      wg[0], f"g_ld_{t + 1}_0", "sGg0")
                b.wait("gpsimd", f"dn_{t}_1_7")
                b.dma("gpsimd", g_sb[1].rearrange("p (kc i) -> p kc i", i=256),
                      wg[1], f"g_ld_{t + 1}_1", "sGg1")
                for iu in (0, 1):
                    b.wait("gpsimd", f"uu_{t}_{(62 + iu) // 2}_{(62 + iu) % 2}")
                    b.dma("gpsimd",
                          u_sb[iu % 2].rearrange("p (kc i) -> p kc i", i=128),
                          wu[iu], f"u_ld_{t + 1}_{iu}", f"sGu{iu % 2}" if iu % 2 else "sGu")

        def script():
            rot1.clear()
            b.memset(ones_col[:], 1.0)
            b.memset(ones128[:], 1.0)
            b.memset(ones_row, 1.0)
            b.memset(vx_sb[:], 1.0, ev="vext_ones")
            b.dma("sync", sg_sb[:], sg, "sg_ld", "sSpre")
            b.dma("sync", su_sb[:], su, "su_ld", "sSpre")
            b.dma("sync", sd_sb[:], sd, "sd_ld", "sSpre")
            b.dma("sync", stg, bq[0:1, 0:1024], "bqs_0", "sSpre")
            b.wait("scalar", "bqs_0")
            b.act(bq_bf[0:1, 0:1024], stg, AF.Copy, ev="bqb_0")
            b.wait("sync", "bqb_0")
            b.dma("sync", stg, bq[0:1, 1024:2048], "bqs_1", "sSpre")
            b.wait("scalar", "bqs_1")
            b.act(bq_bf[0:1, 1024:2048], stg, AF.Copy, ev="bqb_1")
            b.wait("sync", "bqb_1")
            b.dma("sync", stg[0:1, 0:512], bk, "bks", "sSpre")
            b.wait("scalar", "bks")
            b.act(bk_bf, stg[0:1, 0:512], AF.Copy, ev="bkb")
            b.wait("sync", "bkb")
            b.dma("sync", stg[0:1, 0:512], bv, "bvs", "sSpre")
            b.wait("scalar", "bvs")
            b.act(bv_bf, stg[0:1, 0:512], AF.Copy, ev="bvb")
            for t in (0, 1):
                b.dma("gpsimd", xb[t % 2].rearrange("p (c t) -> p c t", t=T),
                      xTr[:, :, t * T:(t + 1) * T], f"xb_{t}", f"sGx{t % 2}")
            b.dma("gpsimd", wk_sb.rearrange("p (kc n) -> p kc n", n=512),
                  wkT.rearrange("(kc p) n -> p kc n", p=128), "wk_ld", "sGpre")
            b.dma("gpsimd", wv_sb.rearrange("p (kc n) -> p kc n", n=512),
                  wvT.rearrange("(kc p) n -> p kc n", p=128), "wv_ld", "sGpre")
            for i in range(4):
                src = wqT.rearrange("(kc p) n -> p kc n", p=128)[:, 4 * i:4 * (i + 1), :]
                dst = wq_sb.rearrange("p (kc n) -> p kc n", n=2048)[:, 4 * i:4 * (i + 1), :]
                b.dma("gpsimd", dst, src, f"wq_ld{i}" if i < 3 else "wq_ld", "sGpre")

            b.waits("tensor", ["bqb_1", "bkb", "bvb", "vext_ones"])

            for t in range(NT + 1):
                if t < NT:
                    b.wait("tensor", f"xb_{t}")
                    ps_sum = acquire1([f"p1_{t}_rm"])
                    for kc in range(NCH):
                        b.mm(ps_sum[0:1, :], ones_col[:], ch(xb[t % 2], kc),
                             start=(kc == 0), stop=(kc == NCH - 1),
                             ev=(f"p1sum_{t}" if kc == NCH - 1 else None))
                    ps_sq = acquire1([f"p1_{t}_rq"])
                    for kc in range(NCH):
                        b.wait("scalar", f"xb_{t}")
                        if kc >= 2:
                            b.wait("scalar", f"p1sq_{t}_{kc - 2}")
                        elif t > 0:
                            b.wait("scalar", f"p1sq_{t - 1}_{kc + 14}")
                        b.act(xsq[kc % 2], ch(xb[t % 2], kc), AF.Square,
                              ev=f"xsq_{t}_{kc}")
                        b.wait("tensor", f"xsq_{t}_{kc}")
                        b.mm(ps_sq[0:1, :], ones_col[:], xsq[kc % 2][:],
                             start=(kc == 0), stop=(kc == NCH - 1),
                             ev=f"p1sq_{t}_{kc}")
                    rows_chain(f"p1_{t}", ps_sum, ps_sq, f"p1sum_{t}",
                               f"p1sq_{t}_{NCH - 1}", r1m, r1q, r1l, r1r,
                               r1rbf, r1nbf)

                if t > 0:
                    _emit_kv_phase(t - 1)
                if t == NT:
                    b.wait("vector", "kv_done")
                    for kk in range(KVH):
                        b.vcopy(kve[:, kk * 129:(kk + 1) * 129],
                                pb[4 + kk][:, 0:129],
                                ev=("kvev" if kk == KVH - 1 else None))
                    b.wait("sync", "kvev")
                    b.dma("sync", cc_in_h.ap(), kve, "cc_st", "sScc")
                    b.wait("gpsimd", "cc_st")
                    b.cc([cc_in_h.ap().opt()], [cc_out_h.ap().opt()],
                         [[0, 1, 2, 3], [4, 5, 6, 7]], ev="cc_done")
                    b.wait("gpsimd", "cc_done")
                    b.dma("gpsimd", kv_sb, cc_out_h.ap(), "kv_ld", "sGkv")
                    if debug:
                        b.dma("gpsimd", cc_dbg_h.ap(), cc_out_h.ap(), "cc_dbg",
                              "sGdbg")

                if t < NT:
                    b.wait("tensor", f"p1_{t}_rbf")
                    pR = acquire1([f"rbx1e_{t}"])
                    b.mm(pR[:], ones128[:], r1rbf, start=True, stop=True,
                         ev=f"bcR1_{t}")
                    b.wait("tensor", f"p1_{t}_rows")
                    pN = acquire1([f"nbx1e_{t}"])
                    b.mm(pN[:], ones128[:], r1nbf, start=True, stop=True,
                         ev=f"bcN1_{t}")
                    b.wait("scalar", f"bcR1_{t}")
                    if t > 0:
                        b.wait("scalar", f"xn_{t - 1}_{NCH - 1}")
                    b.act(rbx1[:], pR[:], AF.Copy, ev=f"rbx1e_{t}")
                    b.wait("scalar", f"bcN1_{t}")
                    b.act(nbx1[:], pN[:], AF.Copy, ev=f"nbx1e_{t}")

                if t > 0:
                    if t == NT:
                        _emit_p2_loads(0)
                    for m in range(16):
                        _emit_q_group(t - 1, m)
                    b.wait("sync", f"sqa_{t - 1}_15")
                    b.dma("sync", sq_sp[:, (t - 1) * 8192:t * 8192], sq1[:],
                          f"st_sq_{t - 1}", "sSsq")

                if t < NT:
                    b.waits("vector", [f"rbx1e_{t}", f"nbx1e_{t}"])
                    if t > 0:
                        b.wait("vector", f"q_{t - 1}_15")
                    for kc in range(NCH):
                        b.tt(apt1[kc % 2], ch(xb[t % 2], kc), rbx1[:], ALU.mult,
                             ev=f"apm_{t}_{kc}")
                        b.wait("vector", f"apm_{t}_{kc}")
                        b.tt(ch(xn, kc), apt1[kc % 2][:], nbx1[:], ALU.add,
                             ev=f"xn_{t}_{kc}")

                if t > 0:
                    if t + 1 < NT:
                        b.waits("gpsimd", [f"xn_{t - 1}_{NCH - 1}",
                                           f"xsq_{t - 1}_{NCH - 1}"])
                        b.dma("gpsimd", xb[(t + 1) % 2].rearrange(
                            "p (c t) -> p c t", t=T),
                            xTr[:, :, (t + 1) * T:(t + 2) * T], f"xb_{t + 1}",
                            f"sGx{(t + 1) % 2}")

            if stop_after == "p1":
                b.wait("sync", f"st_sq_{NT - 1}")
                b.wait("sync", "kv_done")
                return
            b.wait("gpsimd", "q_7_15")
            for i in range(4):
                src = woT.rearrange("(kc p) n -> p kc n", p=128)[:, 4 * i:4 * (i + 1), :]
                dst = wo_sb.rearrange("p (kc n) -> p kc n", n=2048)[:, 4 * i:4 * (i + 1), :]
                b.dma("gpsimd", dst, src, f"wo_ld{i}" if i < 3 else "wo_ld", f"sGwo{i % 2}")

            if stop_after == "cc":
                b.wait("sync", "kv_ld")
                b.wait("sync", "wo_ld")
                return
            for h in range(H):
                _head_front(0, h)
                if h > 0:
                    _head_back(0, h - 1)
            _head_back(0, H - 1)
            for t in range(NT):
                if t + 1 < NT:
                    _emit_p2_loads(t + 1)
                for k in range(NCH):
                    if k > 0 or t == 0:
                        _op_step(t, k)
                    if k > 0:
                        _op_stats(t, k - 1)
                    if t > 0:
                        _apply_step(t - 1, k)
                    if t + 1 < NT:
                        _head_front(t + 1, k)
                        if k > 0:
                            _head_back(t + 1, k - 1)
                _op_stats(t, NCH - 1)
                if t + 1 < NT:
                    _head_back(t + 1, NCH - 1)
                    _op_step(t + 1, 0)
                if t > 0:
                    b.wait("sync", f"x1c_{t - 1}_{NCH - 1}")
                else:
                    b.waits("sync", [f"xn_7_{NCH - 1}", f"xsq_7_{NCH - 1}",
                                     "sqa_7_15"])
                b.dma("sync", x2in.rearrange("p (c t) -> p c t", t=T),
                      xTr[:, :, t * T:(t + 1) * T], f"x_{t}", "sSx")
                _emit_pass2_tail(t)
            b.waits("gpsimd", [f"op_{NT - 1}_{NCH - 1}", "stx1_0_12",
                               "stx1_0_13", f"stx1_0_{NCH - 2}",
                               f"stx1_0_{NCH - 1}"])
            x1r = x1_sp.rearrange("p (tt ct) -> p tt ct", ct=512)
            b.dma("gpsimd", x1b.rearrange("p (c t) -> p c t", t=T)[:, 8:16, :],
                  x1r[:, 8:16, :], "x1b_0a", "sGx1b")
            b.wait("gpsimd", f"nm_{NT - 1}_{H - 1}")
            for j in (0, 1):
                b.dma("gpsimd", g_sb[j].rearrange("p (kc i) -> p kc i", i=256),
                      wg[j], f"g_ld_0_{j}", f"sGg{j}")
            for iu in (0, 1):
                b.dma("gpsimd", u_sb[iu % 2].rearrange("p (kc i) -> p kc i", i=128),
                      wu[iu], f"u_ld_0_{iu}", f"sGu{iu % 2}" if iu % 2 else "sGu")
            b.waits("gpsimd", [f"bcN2_{NT - 1}", f"rec_{NT - 1}_{H - 1}"])
            b.dma("gpsimd", x1b.rearrange("p (c t) -> p c t", t=T)[:, 0:8, :],
                  x1r[:, 0:8, :], "x1b_0b", "sGx1b2")
            cs0 = list(range(8, 16)) + list(range(8))
            for kc in range(NCH):
                _apply_step(NT - 1, kc)
                if kc >= 6:
                    _x1_stats_chunk(0, cs0[kc - 6], idx=kc - 6)
            for idx in range(10, NCH):
                _x1_stats_chunk(0, cs0[idx], idx=idx)
            _x1_rows(0)
            _x1_bcast(0)

            if stop_after == "p2":
                b.wait("sync", f"stx1_{NT - 1}_{NCH - 1}")
                return
            b.waits("scalar", ["sg_ld", "su_ld", "sd_ld"])
            _x1_ln_apply(0)
            b.wait("sync", f"x1c_{NT - 1}_{NCH - 1}")
            b.dma("sync", x1f[:], x1_sp[:, 0:8192], "x1f_0", "sSx1f")

            for t in range(NT):
                _emit_pass3_tile(t)
            _d_stats(NT - 1)
            _d_bcast(NT - 1)
            for c in range(NCH):
                _d_apply_chunk(NT - 1, c)

            for c in range(NCH - 4, NCH):
                b.wait("sync", f"st_out_{NT - 1}_{c}")
            if debug:
                b.wait("sync", "cc_dbg")

        script()
        bld.check()
        bld.dry = False

        def mk_body(role):
            def body(eng):
                bld.begin(role, eng)
                script()
            return body

        block.sync(mk_body("sync"))
        block.gpsimd(mk_body("gpsimd"))
        block.tensor(mk_body("tensor"))
        block.scalar(mk_body("scalar"))
        block.vector(mk_body("vector"))

    return nc


def _prep_inputs(inputs):
    x = np.asarray(inputs["L0_attn_hidden_states"])
    wq = np.asarray(inputs["L0_attn_Wq"])
    wk = np.asarray(inputs["L0_attn_Wk"])
    wv = np.asarray(inputs["L0_attn_Wv"])
    wo = np.asarray(inputs["L0_attn_Wo"])
    w_gate = np.asarray(inputs["L0_ffn_w_gate"])
    w_up = np.asarray(inputs["L0_ffn_w_up"])
    w_down = np.asarray(inputs["L0_ffn_w_down"])

    common = dict(
        wqT=np.ascontiguousarray(wq.T),
        wkT=np.ascontiguousarray(wk.T),
        wvT=np.ascontiguousarray(wv.T),
        woT=np.ascontiguousarray(wo.T),
        bq=np.ascontiguousarray(np.asarray(inputs["L0_attn_bq"]).reshape(1, 2048)),
        bk=np.ascontiguousarray(np.asarray(inputs["L0_attn_bk"]).reshape(1, 512)),
        bv=np.ascontiguousarray(np.asarray(inputs["L0_attn_bv"]).reshape(1, 512)),
        wg=np.ascontiguousarray(
            w_gate.reshape(16, 128, 32, 256).transpose(2, 1, 0, 3)),
        wu=np.ascontiguousarray(
            w_up.reshape(16, 128, 64, 128).transpose(2, 1, 0, 3)),
        wd=np.ascontiguousarray(w_down),
        sg=np.ascontiguousarray(
            np.asarray(inputs["L0_ffn_s_gate"]).reshape(64, 128).T),
        su=np.ascontiguousarray(
            np.asarray(inputs["L0_ffn_s_up"]).reshape(64, 128).T),
        sd=np.ascontiguousarray(
            np.asarray(inputs["L0_ffn_s_down"]).reshape(16, 128).T),
    )
    in_maps = []
    for c in range(NCORE):
        bi = c // 4
        toff = (c % 4) * NTOK
        m = dict(common)
        m["xT"] = np.ascontiguousarray(x[bi, toff:toff + NTOK, :].T)
        in_maps.append(m)
    return in_maps


def kernel(**inputs):
    in_maps = _prep_inputs(inputs)
    nc = build_nc()
    res = run_bass_kernel_spmd(nc, in_maps, core_ids=list(range(NCORE)))
    outs = []
    for bi in range(2):
        cols = [np.asarray(res.results[bi * 4 + k]["out"]) for k in range(4)]
        outs.append(np.concatenate(cols, axis=1).T)
    return np.stack(outs, axis=0).astype(np.float32)
